# version 17
# baseline (speedup 1.0000x reference)
import sys
sys.path.insert(0, '/opt/trn_rl_repo')
from contextlib import ExitStack
import math
import numpy as np
import ml_dtypes
import concourse.bass as bass
import concourse.mybir as mybir
from concourse.bass_utils import run_bass_kernel_spmd

F32 = mybir.dt.float32
BF16 = mybir.dt.bfloat16
I32 = mybir.dt.int32
AF = mybir.ActivationFunctionType
ALU = mybir.AluOpType
AX = mybir.AxisListType
BF = ml_dtypes.bfloat16

D = 2048
T = 2048
NKC = 16
NH = 16
HD = 128
NH0 = 44
H1 = 16384
NH1 = 128
EPS = 1e-5
NEG = -1.0e30

SCALE = {0: 1.0 / math.sqrt(HD), 1: 1.0 / HD}
SLOPES = [2.0 ** (-0.5 * (i + 1)) for i in range(NH)]

ENGINES = ("sync", "gpsimd", "tensor", "scalar", "vector")



def _block_in_out(w, nm):
    return np.ascontiguousarray(w.reshape(NKC, 128, nm, 128).transpose(2, 1, 0, 3))


def _block_out_in(w, nh):
    return np.ascontiguousarray(w.reshape(nh, 128, NKC, 128).transpose(0, 3, 2, 1))


def _block_down0(w):
    return np.ascontiguousarray(
        w.reshape(4, 512, NH0 // 2, 2, 128).transpose(0, 2, 4, 3, 1))


def _perm256(a):
    n = a.shape[0]
    return np.ascontiguousarray(
        a.reshape(n // 256, 128, 2, *a.shape[1:]).swapaxes(1, 2).reshape(a.shape))


def _colT(v):
    return np.ascontiguousarray(v.reshape(-1, 128).T)


def host_prep(inp):
    p = {}
    p["w0qkv"] = _block_in_out(np.asarray(inp["L0_attn_w_attn"], np.float32), 48)
    p["b0qkv"] = _colT(np.asarray(inp["L0_attn_b_attn"], np.float32))
    p["w0proj"] = _block_in_out(np.asarray(inp["L0_attn_w_proj"], np.float32), 16)
    p["b0proj"] = _colT(np.asarray(inp["L0_attn_b_proj"], np.float32))
    p["w1qkv"] = _block_in_out(np.asarray(inp["L1_attn_w_qkv"], np.float32), 48)
    p["b1qkv"] = _colT(np.asarray(inp["L1_attn_b_qkv"], np.float32))
    p["w1proj"] = _block_in_out(np.asarray(inp["L1_attn_w_proj"], np.float32), 16)
    p["b1proj"] = _colT(np.asarray(inp["L1_attn_b_proj"], np.float32))

    p["wg0"] = _block_out_in(np.asarray(inp["L0_ffn_wq_gate"], np.int32), NH0)
    p["wu0"] = _block_out_in(np.asarray(inp["L0_ffn_wq_up"], np.int32), NH0)
    p["wd0"] = _block_down0(np.asarray(inp["L0_ffn_wq_down"], np.int32))
    p["sg0"] = _colT(np.asarray(inp["L0_ffn_s_gate"], np.float32))
    p["su0"] = _colT(np.asarray(inp["L0_ffn_s_up"], np.float32))
    p["sd0"] = _colT(np.asarray(inp["L0_ffn_s_down"], np.float32))

    gp = _perm256(np.asarray(inp["L1_ffn_gate_W_packed"], np.int32))
    up = _perm256(np.asarray(inp["L1_ffn_up_W_packed"], np.int32))
    gs = _perm256(np.asarray(inp["L1_ffn_gate_scales"], np.float32))
    us = _perm256(np.asarray(inp["L1_ffn_up_scales"], np.float32))
    p["gpT"] = np.ascontiguousarray(gp.T)
    p["upT"] = np.ascontiguousarray(up.T)
    p["gsT"] = np.ascontiguousarray(gs.T)
    p["usT"] = np.ascontiguousarray(us.T)
    p["dpT"] = np.ascontiguousarray(np.asarray(inp["L1_ffn_down_W_packed"], np.int32).T)
    p["dsT"] = np.ascontiguousarray(np.asarray(inp["L1_ffn_down_scales"], np.float32).T)

    p["ident_bf"] = np.eye(128, dtype=BF)
    p["ident_f"] = np.eye(128, dtype=np.float32)
    p["ones_bf"] = np.ones((128, 1), dtype=BF)
    p["onesr_f"] = np.ones((1, 128), dtype=np.float32)
    jj, pp = np.meshgrid(np.arange(512), np.arange(128))
    p["relT"] = (pp - jj).astype(np.float32)
    p["relT2"] = np.concatenate([p["relT"], p["relT"] + 128.0], axis=1)
    c, r = np.meshgrid(np.arange(128), np.arange(128))
    p["masktriT"] = np.where(r > c, NEG, 0.0).astype(np.float32)
    return p



class Sem:
    def __init__(self, st, nc, name):
        self.h = st.enter_context(nc.semaphore(name))
        self.v = 0

    def inc(self, n=1):
        self.v += n
        return self.v


class B:
    def __init__(self, nc, st):
        self.nc = nc
        self.st = st
        self.ops = {e: [] for e in ENGINES}
        self.sQ = Sem(st, nc, "sQ")
        self.sG = Sem(st, nc, "sG")
        self.q = [Sem(st, nc, f"q{i}") for i in range(8)]
        self.g = [Sem(st, nc, f"g{i}") for i in range(4)]
        self.sPE = Sem(st, nc, "sPE")
        self.sAC = Sem(st, nc, "sAC")
        self.sDV = Sem(st, nc, "sDV")
        self.pb = Sem(st, nc, "pb")

    def sbuf(self, name, shape, dt):
        return self.st.enter_context(self.nc.sbuf_tensor(name, shape, dt))

    def psum(self, name, shape, dt):
        return self.st.enter_context(self.nc.psum_tensor(name, shape, dt))

    def em(self, eng, fn):
        self.ops[eng].append(fn)

    def wait(self, eng, sem, v):
        if v and v > 0:
            self.em(eng, lambda e, h=sem.h, vv=v: e.wait_ge(h, vv))

    def dma(self, eng, dst, src, sem):
        v = sem.inc(16)
        self.em(eng, lambda e, d=dst, s=src, h=sem.h: e.dma_start(d, s).then_inc(h, 16))
        return v

    def op(self, eng, fn, sem=None, k=1):
        if eng == "vector" or eng == "scalar":
            s = self.sDV if eng == "vector" else self.sAC
            assert sem is None or sem is s
            if s.v > 0:
                self.em(eng, lambda e, h=s.h, vv=s.v: e.wait_ge(h, vv))
            v = s.inc(1)
            self.em(eng, lambda e, f=fn, h=s.h: f(e).then_inc(h, 1))
            return v
        if sem is None:
            self.em(eng, fn)
            return None
        v = sem.inc(k)
        self.em(eng, lambda e, f=fn, h=sem.h, kk=k: f(e).then_inc(h, kk))
        return v

    def barrier(self):
        sems = [self.sQ, self.sG, self.sPE, self.sAC, self.sDV] + self.q + self.g
        vals = [s.v for s in sems]
        target = self.pb.v + len(ENGINES)
        self.pb.v = target
        for eng in ENGINES:
            def _bar(e, sems=sems, vals=vals, pbh=self.pb.h, tgt=target):
                for s, v in zip(sems, vals):
                    if v > 0:
                        e.wait_ge(s.h, v)
                e.nop().then_inc(pbh, 1)
                e.wait_ge(pbh, tgt)
            self.em(eng, _bar)

    def finish(self):
        block = self.st.enter_context(self.nc.Block())
        for eng in ENGINES:
            body = getattr(block, eng)
            lst = self.ops[eng]

            def _run(e, lst=lst):
                for fn in lst:
                    fn(e)
            body(_run)


class Arena:

    def __init__(self, b: B):
        self.big = b.sbuf("big", [128, NKC * T], BF16)
        self.stb = b.sbuf("stb", [128, 8192], F32)
        self.ptb = b.sbuf("ptb", [128, 8192], BF16)
        self.xs = [b.sbuf(f"xs{i}", [128, T], F32) for i in range(2)]
        self.osg = [b.sbuf(f"osg{i}", [128, T], F32) for i in range(2)]
        self.qb = [b.sbuf(f"qbuf{i}", [128, T], BF16) for i in range(2)]
        self.wst2 = [b.sbuf(f"wst2{i}", [128, T], BF16) for i in range(2)]
        self.gs = [b.sbuf(f"gsb{i}", [128, 512], F32) for i in range(2)]
        self.stats = b.sbuf("stats", [1, 4 * T], F32)
        self.tmpf = b.sbuf("tmpf", [128, 512], F32)
        self.small = b.sbuf("small", [128, 40], F32)
        self.ps01 = b.psum("ps01", [128, 1024], F32)
        self.ps = [self.ps01[:, 0:512], self.ps01[:, 512:1024]] + \
            [b.psum(f"ps{i}", [128, 512], F32) for i in range(2, 8)]
        self.ident_bf = b.sbuf("identbf", [128, 128], BF16)
        self.ident_f = b.sbuf("identf", [128, 128], F32)
        self.ones_bf = b.sbuf("onesbf", [128, 1], BF16)
        self.onesr_f = b.sbuf("onesrf", [1, 128], F32)
        self.relT = b.sbuf("relTb", [128, 512], F32)
        self.relT2 = b.sbuf("relT2b", [128, 1024], F32)
        self.mtri = b.sbuf("mtrib", [128, 128], F32)


def load_consts(b: B, a: Arena, dt):
    v = 0
    for nm, buf in (("ident_bf", a.ident_bf), ("ident_f", a.ident_f),
                    ("ones_bf", a.ones_bf), ("onesr_f", a.onesr_f),
                    ("relT", a.relT), ("relT2", a.relT2), ("masktriT", a.mtri)):
        v = b.dma("sync", buf[:, :], dt[nm].ap(), b.q[6])
    return v



def ph_ln(b: B, a: Arena, xdram):
    sq = a.qb[0]
    xld = [b.dma("sync", a.xs[kc][:, :], xdram[kc * 128:(kc + 1) * 128, :], b.q[kc])
           for kc in range(2)]
    pe_kc = [0] * NKC
    for kc in range(NKC):
        b.wait("scalar", b.q[kc % 2], xld[kc])
        if kc >= 1:
            b.wait("scalar", b.sPE, pe_kc[kc - 1])
        bigc = a.big[:, kc * T:(kc + 1) * T]
        b.op("scalar", lambda e, o=bigc, i=a.xs[kc % 2]: e.activation(
            o, i[:, :], AF.Identity))
        v = b.op("scalar", lambda e, i=a.xs[kc % 2]: e.activation(
            sq[:, :], i[:, :], AF.Square), b.sAC)
        if kc + 2 < NKC:
            b.wait("sync", b.sAC, v)
            xld.append(b.dma("sync", a.xs[kc % 2][:, :],
                             xdram[(kc + 2) * 128:(kc + 3) * 128, :], b.q[kc % 2]))
        b.wait("tensor", b.sAC, v)
        for w in range(4):
            b.op("tensor", lambda e, o=bigc, w_=w, s=(kc == 0), t=(kc == NKC - 1):
                 e.matmul(a.ps[w_][0:1, :], a.ones_bf[:, :],
                          o[:, w_ * 512:(w_ + 1) * 512], start=s, stop=t))
            pe_kc[kc] = b.op("tensor", lambda e, w_=w, s=(kc == 0), t=(kc == NKC - 1):
                             e.matmul(a.ps[4 + w_][0:1, :], a.ones_bf[:, :],
                                      sq[:, w_ * 512:(w_ + 1) * 512], start=s, stop=t),
                             b.sPE)
    b.wait("vector", b.sPE, pe_kc[NKC - 1])
    for w in range(4):
        b.op("vector", lambda e, w_=w: e.tensor_copy(
            a.stats[0:1, w_ * 512:(w_ + 1) * 512], a.ps[w_][0:1, :]))
        b.op("vector", lambda e, w_=w: e.tensor_copy(
            a.stats[0:1, T + w_ * 512:T + (w_ + 1) * 512], a.ps[4 + w_][0:1, :]))
    b.op("vector", lambda e: e.tensor_scalar(a.stats[0:1, 0:T], a.stats[0:1, 0:T],
                                             1.0 / D, None, ALU.mult))
    b.op("vector", lambda e: e.tensor_scalar(a.stats[0:1, T:2 * T], a.stats[0:1, T:2 * T],
                                             1.0 / D, None, ALU.mult))
    b.op("vector", lambda e: e.tensor_tensor(a.stats[0:1, 2 * T:3 * T], a.stats[0:1, 0:T],
                                             a.stats[0:1, 0:T], ALU.mult))
    b.op("vector", lambda e: e.tensor_tensor(a.stats[0:1, T:2 * T], a.stats[0:1, T:2 * T],
                                             a.stats[0:1, 2 * T:3 * T], ALU.subtract))
    dv = b.op("vector", lambda e: e.tensor_scalar(a.stats[0:1, T:2 * T],
                                                  a.stats[0:1, T:2 * T], EPS, None,
                                                  ALU.add), b.sDV)
    b.wait("scalar", b.sDV, dv)
    av = b.op("scalar", lambda e: e.activation(a.stats[0:1, 2 * T:3 * T], a.stats[0:1, T:2 * T],
                                               AF.Sqrt), b.sAC)
    b.wait("vector", b.sAC, av)
    b.op("vector", lambda e: e.reciprocal(a.stats[0:1, 2 * T:3 * T], a.stats[0:1, 2 * T:3 * T]))
    dv = b.op("vector", lambda e: e.tensor_tensor(a.stats[0:1, 3 * T:4 * T], a.stats[0:1, 0:T],
                                                  a.stats[0:1, 2 * T:3 * T], ALU.mult), b.sDV)
    b.wait("tensor", b.sDV, dv)
    pv = None
    for w in range(4):
        b.op("tensor", lambda e, w_=w: e.matmul(
            a.ps[w_][:, :], a.onesr_f[:, :], a.stats[0:1, 2 * T + w_ * 512:2 * T + (w_ + 1) * 512],
            start=True, stop=True))
        pv = b.op("tensor", lambda e, w_=w: e.matmul(
            a.ps[4 + w_][:, :], a.onesr_f[:, :], a.stats[0:1, 3 * T + w_ * 512:3 * T + (w_ + 1) * 512],
            start=True, stop=True), b.sPE if w == 3 else None)
    b.wait("vector", b.sPE, pv)
    for w in range(4):
        b.op("vector", lambda e, w_=w: e.tensor_copy(
            a.stb[:, w_ * 512:(w_ + 1) * 512], a.ps[w_][:, :]))
        b.op("vector", lambda e, w_=w: e.tensor_copy(
            a.stb[:, 2048 + w_ * 512:2048 + (w_ + 1) * 512], a.ps[4 + w_][:, :]))
    tmp2 = a.stb[:, 4096:6144]
    for kc in range(NKC):
        sl = slice(kc * T, (kc + 1) * T)
        b.op("vector", lambda e, s=sl: e.tensor_tensor(
            tmp2, a.big[:, s], a.stb[:, 0:2048], ALU.mult))
        b.op("vector", lambda e, s=sl: e.tensor_tensor(
            a.big[:, s], tmp2, a.stb[:, 2048:4096], ALU.subtract))
    b.barrier()


def ph_qkv(b: B, a: Arena, wdram, bcol, qkvT):
    wstrip = a.wst2
    mm_done = [0] * 48
    ev_done = [0] * 48
    st_done = [0] * 48
    for m in range(48):
        if m >= 2:
            b.wait("gpsimd", b.sPE, mm_done[m - 2])
        wld = b.dma("gpsimd",
                    wstrip[m % 2][:, :].rearrange("p (k c) -> p k c", k=NKC),
                    wdram[m], b.g[m % 2])
        b.wait("tensor", b.g[m % 2], wld)
        if m >= 2:
            b.wait("tensor", b.sAC, ev_done[m - 2])
        base = 0 if m % 2 == 0 else 4
        v = None
        for kc in range(NKC):
            for w in range(4):
                v = b.op("tensor", lambda e, w_=w, kc_=kc, m_=m, ba=base: e.matmul(
                    a.ps[ba + w_][:, :],
                    wstrip[m_ % 2][:, kc_ * 128:(kc_ + 1) * 128],
                    a.big[:, kc_ * T + w_ * 512: kc_ * T + (w_ + 1) * 512],
                    start=(kc_ == 0), stop=(kc_ == NKC - 1)),
                    b.sPE if kc == NKC - 1 and w == 3 else None)
        mm_done[m] = v
        b.wait("scalar", b.sPE, v)
        if m >= 2:
            b.wait("scalar", b.q[4 + m % 2], st_done[m - 2])
        for w in range(4):
            v = b.op("scalar", lambda e, w_=w, m_=m, ba=base: e.activation(
                a.qb[m_ % 2][:, w_ * 512:(w_ + 1) * 512], a.ps[ba + w_][:, :],
                AF.Identity, bias=bcol[:, m_:m_ + 1]),
                b.sAC if w == 3 else None)
        ev_done[m] = v
        b.wait("sync", b.sAC, v)
        st_done[m] = b.dma("sync", qkvT[m, :, :], a.qb[m % 2][:, :], b.q[4 + m % 2])
    b.barrier()


def ph_attn(b: B, a: Arena, layer, qkvT, ab):
    qkvh, vtm, rlb, rlrow, gsc, aph = ab
    s = SCALE[layer]
    trps = a.ps[2]
    PS_L, PS_Y, PS_RL = 3, 4, 5
    cur_ld = [b.dma("sync", qkvh[0][t], qkvT[16 * t, :, :], b.q[0]) for t in range(3)]
    head_pe_done = 0
    consumed = {0: (b.sPE, 0), 1: (b.sPE, 0)}
    final_tt = 0
    vtm_copy = 0
    for h in range(NH):
        cur = qkvh[h % 2]
        nxt = qkvh[(h + 1) % 2]
        my_ld = cur_ld
        if h + 1 < NH:
            if h >= 1:
                b.wait("sync", b.sPE, head_pe_done)
            cur_ld = [b.dma("sync", nxt[t], qkvT[16 * t + h + 1, :, :], b.q[(h + 1) % 2])
                      for t in range(3)]
        qh, kh, vh = cur[0], cur[1], cur[2]
        b.wait("tensor", b.q[h % 2], my_ld[2])
        b.wait("vector", b.q[h % 2], my_ld[2])
        for g4 in range(4):
            if h > 0 and g4 == 0:
                b.wait("tensor", b.sDV, final_tt)
            cv = b.op("vector", lambda e, g4_=g4, vh_=vh: e.tensor_copy(
                a.tmpf[:, :], vh_[:, g4_ * 512:(g4_ + 1) * 512]), b.sDV)
            b.wait("tensor", b.sDV, cv)
            if g4 > 0:
                b.wait("tensor", b.sDV, vtm_copy)
            v = None
            for i4 in range(4):
                v = b.op("tensor", lambda e, i4_=i4: e.transpose(
                    trps[:, i4_ * 128:(i4_ + 1) * 128],
                    a.tmpf[:, i4_ * 128:(i4_ + 1) * 128], a.ident_f[:, :]),
                    b.sPE if i4 == 3 else None)
            b.wait("vector", b.sPE, v)
            vtm_copy = b.op("vector", lambda e, g4_=g4: e.tensor_copy(
                vtm[:, g4_ * 512:(g4_ + 1) * 512], trps[:, :]), b.sDV)
        if layer == 1:
            b.op("vector", lambda e, sl=SLOPES[h] / s: e.tensor_scalar(
                aph[:, :], a.relT2[:, :], sl, None, ALU.mult))
        for qb in range(4):
            nk = 4 * qb + 4
            expv = None
            exps = [None] * nk

            def emit_ly(kc, nk=nk):
                b.wait("tensor", b.sAC, exps[kc])
                if kc == 0:
                    b.wait("tensor", b.sDV, final_tt)
                pt = a.ptb[:, kc * 512:(kc + 1) * 512]
                b.op("tensor", lambda e, pt_=pt, s_=(kc == 0), t_=(kc == nk - 1):
                     e.matmul(a.ps[PS_L][0:1, :], a.ones_bf[:, :], pt_,
                              start=s_, stop=t_))
                return b.op("tensor", lambda e, pt_=pt, kc_=kc, s_=(kc == 0),
                            t_=(kc == nk - 1): e.matmul(
                                a.ps[PS_Y][:, :], vtm[:, kc_ * 128:(kc_ + 1) * 128],
                                pt_, start=s_, stop=t_),
                            b.sPE if kc == nk - 1 else None)

            for kc in range(nk):
                par = kc % 2
                psb = a.ps[par]
                csem, cval = consumed[par]
                b.wait("tensor", csem, cval)
                v = b.op("tensor", lambda e, ps=psb, kc_=kc, qb_=qb, kh_=kh, qh_=qh:
                         e.matmul(ps[:, :], kh_[:, kc_ * 128:(kc_ + 1) * 128],
                                  qh_[:, qb_ * 512:(qb_ + 1) * 512],
                                  start=True, stop=True), b.sPE)
                diag = kc >= 4 * qb
                off = 128 * (kc - 4 * qb) if diag else 0
                if layer == 1:
                    b.wait("vector", b.sPE, v)
                    st = a.stb[:, kc * 512:(kc + 1) * 512]
                    Cs = SLOPES[h] * (128 * kc - 512 * qb) / s
                    mk = b.op("vector", lambda e, st_=st, ps=psb, C_=Cs:
                              e.scalar_tensor_tensor(
                                  st_, ps[:, :], C_, aph[:, 0:512],
                                  ALU.add, ALU.add), b.sDV)
                    consumed[par] = (b.sDV, mk)
                    if diag:
                        if off > 0:
                            b.op("vector", lambda e, st_=st, o=off:
                                 e.memset(st_[:, 0:o], NEG))
                        mk = b.op("vector", lambda e, st_=st, o=off:
                                  e.tensor_tensor(
                                      st_[:, o:o + 128], st_[:, o:o + 128],
                                      a.mtri[:, :], ALU.add), b.sDV)
                    b.wait("scalar", b.sDV, mk)
                    expv = b.op("scalar", lambda e, kc_=kc, st_=st: e.activation(
                        a.ptb[:, kc_ * 512:(kc_ + 1) * 512], st_, AF.Exp, scale=s),
                        b.sAC)
                    exps[kc] = expv
                else:
                    if diag:
                        b.wait("vector", b.sPE, v)
                        if off > 0:
                            b.op("vector", lambda e, ps=psb, o=off:
                                 e.memset(ps[:, 0:o], NEG))
                        mk = b.op("vector", lambda e, ps=psb, o=off: e.tensor_tensor(
                            ps[:, o:o + 128], ps[:, o:o + 128], a.mtri[:, :],
                            ALU.add), b.sDV)
                        b.wait("scalar", b.sDV, mk)
                    b.wait("scalar", b.sPE, v)
                    expv = b.op("scalar", lambda e, kc_=kc, ps=psb: e.activation(
                        a.ptb[:, kc_ * 512:(kc_ + 1) * 512], ps[:, :], AF.Exp,
                        scale=s), b.sAC)
                    consumed[par] = (b.sAC, expv)
                    exps[kc] = expv
                if kc >= 2:
                    emit_ly(kc - 2)
            emit_ly(nk - 2)
            v = emit_ly(nk - 1)
            b.wait("vector", b.sPE, v)
            b.op("vector", lambda e, qb_=qb: e.tensor_copy(
                a.stats[0:1, 4096 + qb_ * 512:4096 + (qb_ + 1) * 512],
                a.ps[PS_L][0:1, :]))
            final_tt = b.op("vector", lambda e, h_=h, qb_=qb: e.tensor_copy(
                a.big[:, h_ * T + qb_ * 512: h_ * T + (qb_ + 1) * 512],
                a.ps[PS_Y][:, :]), b.sDV)
        rcpv = b.op("vector", lambda e: e.reciprocal(
            a.stats[0:1, 6144:8192], a.stats[0:1, 4096:6144]), b.sDV)
        rl_done = rcpv
        for qb in range(4):
            b.wait("tensor", b.sDV, rl_done)
            v = b.op("tensor", lambda e, qb_=qb: e.matmul(
                a.ps[PS_RL][:, :], a.onesr_f[:, :],
                a.stats[0:1, 6144 + qb_ * 512:6144 + (qb_ + 1) * 512],
                start=True, stop=True), b.sPE)
            b.wait("vector", b.sPE, v)
            b.op("vector", lambda e: e.tensor_copy(rlb[:, :], a.ps[PS_RL][:, :]))
            rl_done = b.op("vector", lambda e, h_=h, qb_=qb: e.tensor_tensor(
                a.big[:, h_ * T + qb_ * 512: h_ * T + (qb_ + 1) * 512],
                a.big[:, h_ * T + qb_ * 512: h_ * T + (qb_ + 1) * 512],
                rlb[:, :], ALU.mult), b.sDV)
        head_pe_done = b.sPE.v
    b.barrier()


def ph_proj(b: B, a: Arena, wdram, bcol, xdram, xout):
    wstrip = a.wst2
    xld = [0] * 16
    for m in range(2):
        xld[m] = b.dma("sync", a.xs[m][:, :], xdram[m * 128:(m + 1) * 128, :], b.q[m])
    mm_done = [0] * 16
    ev_done = [0] * 16
    dv_done = [0] * 16
    st_done = [0] * 16
    for m in range(16):
        if m >= 2:
            b.wait("gpsimd", b.sPE, mm_done[m - 2])
        wld = b.dma("gpsimd",
                    wstrip[m % 2][:, :].rearrange("p (k c) -> p k c", k=NKC),
                    wdram[m], b.g[m % 2])
        b.wait("tensor", b.g[m % 2], wld)
        if m >= 2:
            b.wait("tensor", b.sAC, ev_done[m - 2])
        base = 0 if m % 2 == 0 else 4
        v = None
        for kc in range(NKC):
            for w in range(4):
                v = b.op("tensor", lambda e, w_=w, kc_=kc, m_=m, ba=base: e.matmul(
                    a.ps[ba + w_][:, :],
                    wstrip[m_ % 2][:, kc_ * 128:(kc_ + 1) * 128],
                    a.big[:, kc_ * T + w_ * 512: kc_ * T + (w_ + 1) * 512],
                    start=(kc_ == 0), stop=(kc_ == NKC - 1)),
                    b.sPE if kc == NKC - 1 and w == 3 else None)
        mm_done[m] = v
        b.wait("scalar", b.sPE, v)
        if m >= 2:
            b.wait("scalar", b.q[4 + m % 2], st_done[m - 2])
        for w in range(4):
            v = b.op("scalar", lambda e, w_=w, m_=m, ba=base: e.activation(
                a.osg[m_ % 2][:, w_ * 512:(w_ + 1) * 512], a.ps[ba + w_][:, :],
                AF.Identity, bias=bcol[:, m_:m_ + 1]),
                b.sAC if w == 3 else None)
        ev_done[m] = v
        b.wait("vector", b.sAC, v)
        b.wait("vector", b.q[m % 2], xld[m])
        dv_done[m] = b.op("vector", lambda e, m_=m: e.tensor_tensor(
            a.osg[m_ % 2][:, :], a.osg[m_ % 2][:, :], a.xs[m_ % 2][:, :], ALU.add),
            b.sDV)
        if m + 2 < 16:
            b.wait("sync", b.sDV, dv_done[m])
            xld[m + 2] = b.dma("sync", a.xs[m % 2][:, :],
                               xdram[(m + 2) * 128:(m + 3) * 128, :], b.q[m % 2])
        b.wait("sync", b.sDV, dv_done[m])
        st_done[m] = b.dma("sync", xout[m * 128:(m + 1) * 128, :],
                           a.osg[m % 2][:, :], b.q[4 + m % 2])
    b.barrier()


def ph_gateup0(b: B, a: Arena, wg, wu, sgcol, sucol, gudram):
    wst = [a.stb[:, 0:2048].bitcast(I32), a.stb[:, 2048:4096].bitcast(I32)]
    wstu = [a.stb[:, 4096:6144].bitcast(I32), a.stb[:, 6144:8192].bitcast(I32)]
    wbg = [a.ptb[:, 0:2048], a.ptb[:, 2048:4096]]
    wbu = [a.ptb[:, 4096:6144], a.ptb[:, 6144:8192]]
    guo = a.qb
    mm_half = {}
    dv_half = {}
    st_done = [0] * NH0
    deq_done = [0] * NH0
    for h in range(NH0):
        if h >= 2:
            b.wait("sync", b.sDV, deq_done[h - 2])
        b.dma("sync", wst[h % 2].rearrange("p (k c) -> p k c", k=NKC),
              wg[h], b.q[h % 2])
        l2 = b.dma("sync", wstu[h % 2].rearrange("p (k c) -> p k c", k=NKC),
                   wu[h], b.q[h % 2])
        b.wait("vector", b.q[h % 2], l2)
        if h >= 2:
            b.wait("vector", b.sPE, mm_half.get((h - 2, 1), 0))
        b.op("vector", lambda e, h_=h: e.tensor_scalar(
            wbg[h_ % 2], wst[h_ % 2], -128, None, ALU.add))
        deq_done[h] = b.op("vector", lambda e, h_=h: e.tensor_scalar(
            wbu[h_ % 2], wstu[h_ % 2], -128, None, ALU.add), b.sDV)
        b.wait("tensor", b.sDV, deq_done[h])
        for half in range(2):
            base = 4 * half
            if h >= 1:
                b.wait("tensor", b.sDV, dv_half.get((h - 1, half), 0))
            v = None
            for kc in range(NKC):
                for w in range(2):
                    tw = 2 * half + w
                    b.op("tensor", lambda e, h_=h, kc_=kc, w_=w, tw_=tw, ba=base:
                         e.matmul(a.ps[ba + w_][:, :],
                                  wbg[h_ % 2][:, kc_ * 128:(kc_ + 1) * 128],
                                  a.big[:, kc_ * T + tw_ * 512: kc_ * T + (tw_ + 1) * 512],
                                  start=(kc_ == 0), stop=(kc_ == NKC - 1)))
                for w in range(2):
                    tw = 2 * half + w
                    v = b.op("tensor", lambda e, h_=h, kc_=kc, w_=w, tw_=tw, ba=base:
                             e.matmul(a.ps[ba + 2 + w_][:, :],
                                      wbu[h_ % 2][:, kc_ * 128:(kc_ + 1) * 128],
                                      a.big[:, kc_ * T + tw_ * 512: kc_ * T + (tw_ + 1) * 512],
                                      start=(kc_ == 0), stop=(kc_ == NKC - 1)),
                             b.sPE if kc == NKC - 1 and w == 1 else None)
            mm_half[(h, half)] = v
            b.wait("scalar", b.sPE, v)
            prev_dv = dv_half.get((h, half - 1), 0) if half == 1 \
                else dv_half.get((h - 1, 1), 0)
            b.wait("scalar", b.sDV, prev_dv)
            v = None
            for w in range(2):
                v = b.op("scalar", lambda e, h_=h, w_=w, ba=base: e.activation(
                    a.gs[w_][:, :], a.ps[ba + w_][:, :], AF.Silu,
                    scale=sgcol[:, h_:h_ + 1]), b.sAC if w == 1 else None)
            b.wait("vector", b.sAC, v)
            if h >= 2 and half == 0:
                b.wait("vector", b.g[h % 2], st_done[h - 2])
            for w in range(2):
                tw = 2 * half + w
                v = b.op("vector", lambda e, h_=h, w_=w, tw_=tw, ba=base:
                         e.scalar_tensor_tensor(
                             guo[h_ % 2][:, tw_ * 512:(tw_ + 1) * 512],
                             a.ps[ba + 2 + w_][:, :], sucol[:, h_:h_ + 1],
                             a.gs[w_][:, :], ALU.mult, ALU.mult),
                         b.sDV if w == 1 else None)
            dv_half[(h, half)] = v
        b.wait("gpsimd", b.sDV, v)
        st_done[h] = b.dma("gpsimd", gudram[h, :, :], guo[h % 2][:, :], b.g[h % 2])
    b.barrier()


def _down_common(b, a, nh, wpair_fn, gudram, xdram, xout, sdcol):
    gub = [a.ptb[:, 0:2048], a.ptb[:, 2048:4096]]
    st_prev = {}
    pg = {0: 0, 1: 0}
    nhp = nh // 2
    gpair = 0
    for w2 in range(2):
        for dg in range(4):
            last_pe = 0
            grp_start_ac = b.sAC.v
            for hp in range(nhp):
                par = gpair % 2
                gpair += 1
                geng = "sync" if par == 0 else "gpsimd"
                gsem = b.q[2] if par == 0 else b.g[2]
                b.wait(geng, b.sPE, pg[par])
                wsem, wv, wpair = wpair_fn(hp, dg, par, pg[par])
                guld = b.dma(geng,
                             gub[par].rearrange("p (h c) -> p h c", h=2),
                             gudram[2 * hp:2 * hp + 2, :,
                                    w2 * 1024:(w2 + 1) * 1024]
                             .rearrange("h p c -> p h c"), gsem)
                b.wait("tensor", wsem, wv)
                b.wait("tensor", gsem, guld)
                if hp == 0:
                    b.wait("tensor", b.sAC, grp_start_ac)
                v = None
                for hi in range(2):
                    h = 2 * hp + hi
                    for d in range(4):
                        for c in range(2):
                            v = b.op("tensor", lambda e, wp=wpair, hi_=hi, h_=h,
                                     d_=d, c_=c, par_=par: e.matmul(
                                         a.ps[2 * d_ + c_][:, :],
                                         wp[:, hi_ * 512 + d_ * 128:
                                            hi_ * 512 + (d_ + 1) * 128],
                                         gub[par_][:, hi_ * 1024 + c_ * 512:
                                                   hi_ * 1024 + (c_ + 1) * 512],
                                         start=(h_ == 0), stop=(h_ == nh - 1)),
                                     b.sPE if hi == 1 and d == 3 and c == 1 else None)
                pg[par] = v
                last_pe = v
            b.wait("scalar", b.sPE, last_pe)
            for d in range(4):
                dout = 4 * dg + d
                b.wait("sync", b.sDV, st_prev.get(("x", d % 2), 0))
                xv = b.dma("sync", a.xs[d % 2][:, 0:1024],
                           xdram[dout * 128:(dout + 1) * 128, w2 * 1024:(w2 + 1) * 1024],
                           b.q[4 + d % 2])
                b.wait("scalar", b.g[d % 2], st_prev.get(("o", d % 2), 0))
                v = None
                for c in range(2):
                    kw = {}
                    if sdcol is not None:
                        kw["scale"] = sdcol[:, dout:dout + 1]
                    v = b.op("scalar", lambda e, d_=d, c_=c, kw_=kw: e.activation(
                        a.osg[d_ % 2][:, c_ * 512:(c_ + 1) * 512],
                        a.ps[2 * d_ + c_][:, :], AF.Identity, **kw_),
                        b.sAC if c == 1 else None)
                b.wait("vector", b.sAC, v)
                b.wait("vector", b.q[4 + d % 2], xv)
                dv = b.op("vector", lambda e, d_=d: e.tensor_tensor(
                    a.osg[d_ % 2][:, 0:1024], a.osg[d_ % 2][:, 0:1024],
                    a.xs[d_ % 2][:, 0:1024], ALU.add), b.sDV)
                st_prev[("x", d % 2)] = dv
                b.wait("gpsimd", b.sDV, dv)
                st_prev[("o", d % 2)] = b.dma(
                    "gpsimd", xout[dout * 128:(dout + 1) * 128, w2 * 1024:(w2 + 1) * 1024],
                    a.osg[d % 2][:, 0:1024], b.g[d % 2])
    b.barrier()


def ph_down0(b: B, a: Arena, wd, sdcol, gudram, xdram, xout):
    wst = [a.stb[:, 0:1024].bitcast(I32), a.stb[:, 1024:2048].bitcast(I32)]
    wbb = [a.ptb[:, 4096:5120], a.ptb[:, 5120:6144]]
    deq_h = {0: 0, 1: 0}

    def wpair(hp, dg, par, pe_guard):
        b.wait("sync", b.sDV, deq_h[par])
        b.dma("sync", wst[par].rearrange("p (h c) -> p h c", h=2),
              wd[dg, hp, :, :, :], b.q[par])
        lv = b.q[par].v
        b.wait("vector", b.q[par], lv)
        b.wait("vector", b.sPE, pe_guard)
        deq_h[par] = b.op("vector", lambda e, p_=par: e.tensor_scalar(
            wbb[p_], wst[p_], -128, None, ALU.add), b.sDV)
        return b.sDV, deq_h[par], wbb[par]

    _down_common(b, a, NH0, wpair, gudram, xdram, xout, sdcol)


def ph_down1(b: B, a: Arena, decd, gudram, xdram, xout):
    wbb = [a.ptb[:, 4096:5120], a.ptb[:, 5120:6144]]

    def wpair(hp, dg, par, pe_guard):
        b.wait("sync", b.sPE, pe_guard)
        v = b.dma("sync", wbb[par].rearrange("p (h c) -> p h c", h=2),
                  decd[2 * hp:2 * hp + 2, :, 512 * dg:512 * (dg + 1)]
                  .rearrange("h p c -> p h c"), b.q[par])
        return b.q[par], v, wbb[par]

    _down_common(b, a, NH1, wpair, gudram, xdram, xout, None)


def ph_decode1(b: B, a: Arena, gpTs, gsTs, upTs, usTs, dpTs, dsTs,
               ing, inu, ind):
    pk = [a.xs[0][:, :].bitcast(I32), a.xs[1][:, :].bitcast(I32)]
    sct = [a.osg[0][:, :], a.osg[1][:, :]]
    t0 = a.stb[:, 0:2048].bitcast(I32)
    t3 = a.stb[:, 2048:4096].bitcast(I32)
    t1 = a.stb[:, 4096:6144].bitcast(I32)
    t2 = a.stb[:, 6144:8192].bitcast(I32)
    bigf = a.big[:, :].bitcast(F32)
    cf = bigf[:, 0:2048]
    sg = bigf[:, 2048:4096]
    v0 = bigf[:, 4096:6144]
    wo = [a.ptb[:, i * 2048:(i + 1) * 2048] for i in range(4)]
    last_par = {0: 0, 1: 0}
    prev_vmul = [0]

    def decode_unit(packed_ap, scale_ap, out_lo, out_hi, parity):
        b.wait("sync", b.sDV, last_par[parity])
        b.dma("sync", pk[parity], packed_ap, b.q[parity])
        ls = b.dma("sync", sct[parity], scale_ap, b.q[parity])
        b.wait("vector", b.q[parity], ls)
        b.wait("vector", b.g[parity], b.g[parity].v)
        for hi in range(2):
            wbuf = wo[2 * hi + parity]
            if hi == 0:
                b.op("vector", lambda e, p_=parity: e.tensor_scalar(
                    t0, pk[p_], 7, 22, ALU.bitwise_and, ALU.logical_shift_left))
                mka = b.op("vector", lambda e, p_=parity: e.tensor_scalar(
                    t3, pk[p_], 8, None, ALU.bitwise_and), b.sDV)
                sgm = -0.25
            else:
                b.op("vector", lambda e, p_=parity: e.tensor_scalar(
                    t0, pk[p_], 0x70, 18, ALU.bitwise_and, ALU.logical_shift_left))
                mka = b.op("vector", lambda e, p_=parity: e.tensor_scalar(
                    t3, pk[p_], 128, None, ALU.bitwise_and), b.sDV)
                sgm = -0.25 / 16.0
            b.wait("scalar", b.sDV, mka)
            b.wait("scalar", b.sDV, prev_vmul[0])
            b.op("scalar", lambda e, sgm_=sgm: e.activation(
                sg, t3, AF.Copy, scale=sgm_, bias=1.0))
            b.op("vector", lambda e: e.tensor_scalar(
                t1, t0, 0x3F000000, None, ALU.add))
            b.op("vector", lambda e: e.tensor_scalar(
                t2, t0, 2 << 22, None, ALU.min))
            mkb = b.op("vector", lambda e: e.tensor_scalar(
                t2, t2, -1, 2 << 22, ALU.mult, ALU.add), b.sDV)
            b.wait("scalar", b.sDV, mkb)
            cv = b.op("scalar", lambda e: e.activation(
                cf, t2, AF.Copy, scale=float(0.25 / (1 << 22))), b.sAC)
            b.wait("vector", b.sAC, cv)
            b.op("vector", lambda e: e.tensor_tensor(
                v0, t1.bitcast(F32), cf, ALU.subtract))
            b.op("vector", lambda e: e.tensor_tensor(v0, v0, sg, ALU.mult))
            stv = b.op("vector", lambda e, wb=wbuf, p_=parity: e.tensor_tensor(
                wb, v0, sct[p_], ALU.mult), b.sDV)
            prev_vmul[0] = stv
            b.wait("gpsimd", b.sDV, stv)
            for dst, src_fn in (out_lo if hi == 0 else out_hi):
                b.dma("gpsimd", dst, src_fn(wbuf), b.g[parity])
        last_par[parity] = prev_vmul[0]

    for pT, sT, slab in ((gpTs, gsTs, ing), (upTs, usTs, inu)):
        slab5 = slab.rearrange("(h r two) (k c) -> two r h k c",
                               h=8, two=2, k=NKC)
        for jj in range(8):
            packed = pT[jj * 128:(jj + 1) * 128, :]
            scl = sT[jj * 8:(jj + 1) * 8, :].unsqueeze(1).broadcast_to([8, 16, 2048])
            out_lo = [
                (slab5[0, :, :, 2 * jj, :],
                 lambda w: w[0:64, :].rearrange("p (h c) -> p h c", h=8)),
                (slab5[0, :, :, 2 * jj + 1, :],
                 lambda w: w[64:128, :].rearrange("p (h c) -> p h c", h=8)),
            ]
            out_hi = [
                (slab5[1, :, :, 2 * jj, :],
                 lambda w: w[0:64, :].rearrange("p (h c) -> p h c", h=8)),
                (slab5[1, :, :, 2 * jj + 1, :],
                 lambda w: w[64:128, :].rearrange("p (h c) -> p h c", h=8)),
            ]
            decode_unit(packed, scl, out_lo, out_hi, jj % 2)
    for jj in range(8):
        packed = dpTs[jj * 128:(jj + 1) * 128, :]
        scl = dsTs[jj * 8:(jj + 1) * 8, :].unsqueeze(1).broadcast_to([8, 16, 2048])
        out_lo = [(ind[256 * jj:256 * jj + 128, :], lambda w: w)]
        out_hi = [(ind[256 * jj + 128:256 * jj + 256, :], lambda w: w)]
        decode_unit(packed, scl, out_lo, out_hi, jj % 2)
    b.wait("gpsimd", b.g[0], b.g[0].v)
    b.wait("gpsimd", b.g[1], b.g[1].v)
    b.barrier()


def ph_gather(b: B, ccsem, pairs):
    for src_ap, dst_ap in pairs:
        b.op("gpsimd", lambda e, s=src_ap, d=dst_ap: e.collective_compute(
            "AllGather", ALU.bypass, replica_groups=[list(range(8))],
            ins=[s], outs=[d]), ccsem)
    b.barrier()


def ph_gateup1(b: B, a: Arena, decg, decu, gudram):
    wgs = [a.stb[:, 0:2048].bitcast(BF16), a.stb[:, 2048:4096].bitcast(BF16)]
    wus = [a.stb[:, 4096:6144].bitcast(BF16), a.stb[:, 6144:8192].bitcast(BF16)]
    guo = a.qb
    mm_last = {}
    dv_half = {}
    st_done = {}
    for hp in range(64):
        eng = "sync" if hp % 2 == 0 else "gpsimd"
        wsem = b.q[0] if hp % 2 == 0 else b.g[3]
        if hp >= 1:
            b.wait(eng, b.sPE, mm_last.get(hp - 1, 0))
        b.dma(eng, wgs[hp % 2], decg[hp], wsem)
        l2 = b.dma(eng, wus[hp % 2], decu[hp], wsem)
        b.wait("tensor", wsem, l2)
        v = None
        dv = None
        for hi in range(2):
            h = hp * 2 + hi
            for half in range(2):
                base = 4 * half
                b.wait("tensor", b.sDV, dv_half.get((h - 1, half), 0))
                for kc in range(NKC):
                    co = kc * 256 + hi * 128
                    for w in range(2):
                        tw = 2 * half + w
                        b.op("tensor", lambda e, hp_=hp, co_=co, kc_=kc, w_=w, tw_=tw,
                             ba=base: e.matmul(
                                 a.ps[ba + w_][:, :], wgs[hp_ % 2][:, co_:co_ + 128],
                                 a.big[:, kc_ * T + tw_ * 512: kc_ * T + (tw_ + 1) * 512],
                                 start=(kc_ == 0), stop=(kc_ == NKC - 1)))
                    for w in range(2):
                        tw = 2 * half + w
                        v = b.op("tensor", lambda e, hp_=hp, co_=co, kc_=kc, w_=w, tw_=tw,
                                 ba=base: e.matmul(
                                     a.ps[ba + 2 + w_][:, :], wus[hp_ % 2][:, co_:co_ + 128],
                                     a.big[:, kc_ * T + tw_ * 512: kc_ * T + (tw_ + 1) * 512],
                                     start=(kc_ == 0), stop=(kc_ == NKC - 1)),
                                 b.sPE if kc == NKC - 1 and w == 1 else None)
                b.wait("scalar", b.sPE, v)
                prev_dv = dv_half.get((h, half - 1), 0) if half == 1 \
                    else dv_half.get((h - 1, 1), 0)
                b.wait("scalar", b.sDV, prev_dv)
                ev = None
                for w in range(2):
                    ev = b.op("scalar", lambda e, w_=w, ba=base: e.activation(
                        a.gs[w_][:, :], a.ps[ba + w_][:, :], AF.Gelu_apprx_tanh),
                        b.sAC if w == 1 else None)
                b.wait("vector", b.sAC, ev)
                if h >= 2 and half == 0:
                    b.wait("vector", b.g[h % 2], st_done.get(h - 2, 0))
                for w in range(2):
                    tw = 2 * half + w
                    dv = b.op("vector", lambda e, h_=h, w_=w, tw_=tw, ba=base:
                              e.tensor_tensor(
                                  guo[h_ % 2][:, tw_ * 512:(tw_ + 1) * 512],
                                  a.ps[ba + 2 + w_][:, :], a.gs[w_][:, :], ALU.mult),
                              b.sDV if w == 1 else None)
                dv_half[(h, half)] = dv
            b.wait("gpsimd", b.sDV, dv)
            st_done[h] = b.dma("gpsimd", gudram[h, :, :], guo[h % 2][:, :], b.g[h % 2])
        mm_last[hp] = v
    b.barrier()



CONST_KEYS = ["ident_bf", "ident_f", "ones_bf", "onesr_f", "relT", "relT2",
              "masktriT"]


def _const_drams(nc):
    d = {}
    d["ident_bf"] = nc.dram_tensor("ident_bf", [128, 128], BF16, kind="ExternalInput")
    d["ident_f"] = nc.dram_tensor("ident_f", [128, 128], F32, kind="ExternalInput")
    d["ones_bf"] = nc.dram_tensor("ones_bf", [128, 1], BF16, kind="ExternalInput")
    d["onesr_f"] = nc.dram_tensor("onesr_f", [1, 128], F32, kind="ExternalInput")
    d["relT"] = nc.dram_tensor("relT", [128, 512], F32, kind="ExternalInput")
    d["relT2"] = nc.dram_tensor("relT2", [128, 1024], F32, kind="ExternalInput")
    d["masktriT"] = nc.dram_tensor("masktriT", [128, 128], F32, kind="ExternalInput")
    return d


def attn_buffers(b, a):
    bufs = [a.xs[0][:, :].bitcast(BF16), a.xs[1][:, :].bitcast(BF16),
            a.osg[0][:, :].bitcast(BF16)]
    qkvh = [[bufs[t][:, i * T:(i + 1) * T] for t in range(3)] for i in range(2)]
    vtm = a.osg[1][:, :].bitcast(BF16)[:, 0:T]
    rlb = a.osg[1][:, 1024:1536]
    aph = a.qb[1][:, :].bitcast(F32)
    rlrow = a.stats[0:1, 0:512]
    gsc = a.stats[0:1, 600:601]
    return qkvh, vtm, rlb, rlrow, gsc, aph


def build_attn_launch(layer):
    nc = bass.Bass("TRN2", target_bir_lowering=False)
    xin = nc.dram_tensor("xin", [D, T], F32, kind="ExternalInput")
    wqkv = nc.dram_tensor("wqkv", [48, 128, NKC, 128], F32, kind="ExternalInput")
    bqkv = nc.dram_tensor("bqkv", [128, 48], F32, kind="ExternalInput")
    wproj = nc.dram_tensor("wproj", [16, 128, NKC, 128], F32, kind="ExternalInput")
    bproj = nc.dram_tensor("bproj", [128, 16], F32, kind="ExternalInput")
    dt = _const_drams(nc)
    qkvT = nc.dram_tensor("qkvT", [48, 128, T], BF16, kind="Internal")
    xout = nc.dram_tensor("xout", [D, T], F32, kind="ExternalOutput")
    with ExitStack() as st:
        b = B(nc, st)
        a = Arena(b)
        ab = attn_buffers(b, a)
        bq = b.sbuf("bq", [128, 48], F32)
        bp = b.sbuf("bp", [128, 16], F32)
        load_consts(b, a, dt)
        b.dma("sync", bq[:, :], bqkv.ap(), b.q[6])
        v = b.dma("sync", bp[:, :], bproj.ap(), b.q[6])
        for e in ENGINES:
            b.wait(e, b.q[6], v)
        ph_ln(b, a, xin.ap())
        ph_qkv(b, a, wqkv.ap(), bq, qkvT.ap())
        ph_attn(b, a, layer, qkvT.ap(), ab)
        ph_proj(b, a, wproj.ap(), bp, xin.ap(), xout.ap())
        b.finish()
    return nc


def build_ffn0_launch():
    nc = bass.Bass("TRN2", target_bir_lowering=False)
    xin = nc.dram_tensor("xin", [D, T], F32, kind="ExternalInput")
    wg = nc.dram_tensor("wg", [NH0, 128, NKC, 128], I32, kind="ExternalInput")
    wu = nc.dram_tensor("wu", [NH0, 128, NKC, 128], I32, kind="ExternalInput")
    wd = nc.dram_tensor("wd", [4, NH0 // 2, 128, 2, 512], I32, kind="ExternalInput")
    sg = nc.dram_tensor("sg", [128, NH0], F32, kind="ExternalInput")
    su = nc.dram_tensor("su", [128, NH0], F32, kind="ExternalInput")
    sd = nc.dram_tensor("sd", [128, 16], F32, kind="ExternalInput")
    dt = _const_drams(nc)
    gu = nc.dram_tensor("gu", [NH0, 128, T], BF16, kind="Internal")
    xout = nc.dram_tensor("xout", [D, T], F32, kind="ExternalOutput")
    with ExitStack() as st:
        b = B(nc, st)
        a = Arena(b)
        sgb = b.sbuf("sgb", [128, NH0], F32)
        sub = b.sbuf("subb", [128, NH0], F32)
        sdb = b.sbuf("sdbb", [128, 16], F32)
        load_consts(b, a, dt)
        b.dma("sync", sgb[:, :], sg.ap(), b.q[6])
        b.dma("sync", sub[:, :], su.ap(), b.q[6])
        v = b.dma("sync", sdb[:, :], sd.ap(), b.q[6])
        for e in ENGINES:
            b.wait(e, b.q[6], v)
        ph_ln(b, a, xin.ap())
        ph_gateup0(b, a, wg.ap(), wu.ap(), sgb, sub, gu.ap())
        ph_down0(b, a, wd.ap(), sdb, gu.ap(), xin.ap(), xout.ap())
        b.finish()
    return nc


def build_ffn1_launch():
    raise NotImplementedError("ffn1 is fused-only now (collaborative decode)")


def build_fused():
    nc = bass.Bass("TRN2", target_bir_lowering=False)
    xin = nc.dram_tensor("xin", [D, T], F32, kind="ExternalInput")
    w0qkv = nc.dram_tensor("w0qkv", [48, 128, NKC, 128], F32, kind="ExternalInput")
    b0qkv = nc.dram_tensor("b0qkv", [128, 48], F32, kind="ExternalInput")
    w0proj = nc.dram_tensor("w0proj", [16, 128, NKC, 128], F32, kind="ExternalInput")
    b0proj = nc.dram_tensor("b0proj", [128, 16], F32, kind="ExternalInput")
    w1qkv = nc.dram_tensor("w1qkv", [48, 128, NKC, 128], F32, kind="ExternalInput")
    b1qkv = nc.dram_tensor("b1qkv", [128, 48], F32, kind="ExternalInput")
    w1proj = nc.dram_tensor("w1proj", [16, 128, NKC, 128], F32, kind="ExternalInput")
    b1proj = nc.dram_tensor("b1proj", [128, 16], F32, kind="ExternalInput")
    wg = nc.dram_tensor("wg", [NH0, 128, NKC, 128], I32, kind="ExternalInput")
    wu = nc.dram_tensor("wu", [NH0, 128, NKC, 128], I32, kind="ExternalInput")
    wd = nc.dram_tensor("wd", [4, NH0 // 2, 128, 2, 512], I32, kind="ExternalInput")
    sg = nc.dram_tensor("sg", [128, NH0], F32, kind="ExternalInput")
    su = nc.dram_tensor("su", [128, NH0], F32, kind="ExternalInput")
    sd = nc.dram_tensor("sd", [128, 16], F32, kind="ExternalInput")
    gpTs = nc.dram_tensor("gpTs", [1024, 2048], I32, kind="ExternalInput")
    upTs = nc.dram_tensor("upTs", [1024, 2048], I32, kind="ExternalInput")
    gsTs = nc.dram_tensor("gsTs", [64, 2048], F32, kind="ExternalInput")
    usTs = nc.dram_tensor("usTs", [64, 2048], F32, kind="ExternalInput")
    dpTs = nc.dram_tensor("dpTs", [1024, D], I32, kind="ExternalInput")
    dsTs = nc.dram_tensor("dsTs", [64, D], F32, kind="ExternalInput")
    dt = _const_drams(nc)
    qkvT = nc.dram_tensor("qkvT", [48, 128, T], BF16, kind="Internal")
    gu0 = nc.dram_tensor("gu0", [NH0, 128, T], BF16, kind="Internal")
    gu1 = nc.dram_tensor("gu1", [NH1, 128, T], BF16, kind="Internal")
    ing = nc.dram_tensor("ing", [1024, 4096], BF16, kind="Internal")
    inu = nc.dram_tensor("inu", [1024, 4096], BF16, kind="Internal")
    ind = nc.dram_tensor("ind", [2048, 2048], BF16, kind="Internal")
    decg = nc.dram_tensor("decg", [8192, 4096], BF16, kind="Internal")
    decu = nc.dram_tensor("decu", [8192, 4096], BF16, kind="Internal")
    decd = nc.dram_tensor("decd", [16384, 2048], BF16, kind="Internal")
    x1 = nc.dram_tensor("x1", [D, T], F32, kind="Internal")
    x2 = nc.dram_tensor("x2", [D, T], F32, kind="Internal")
    x3 = nc.dram_tensor("x3", [D, T], F32, kind="Internal")
    xout = nc.dram_tensor("xout", [D, T], F32, kind="ExternalOutput")
    with ExitStack() as st:
        b = B(nc, st)
        a = Arena(b)
        ab = attn_buffers(b, a)
        bq0 = b.sbuf("bq0", [128, 48], F32)
        bp0 = b.sbuf("bp0", [128, 16], F32)
        bq1 = b.sbuf("bq1", [128, 48], F32)
        bp1 = b.sbuf("bp1", [128, 16], F32)
        sgb = b.sbuf("sgb", [128, NH0], F32)
        sub = b.sbuf("subb", [128, NH0], F32)
        sdb = b.sbuf("sdbb", [128, 16], F32)
        load_consts(b, a, dt)
        for bf, dr in ((bq0, b0qkv), (bp0, b0proj), (bq1, b1qkv), (bp1, b1proj),
                       (sgb, sg), (sub, su), (sdb, sd)):
            v = b.dma("sync", bf[:, :], dr.ap(), b.q[6])
        for e in ENGINES:
            b.wait(e, b.q[6], v)
        ccsem = Sem(st, nc, "ccsem")
        ph_decode1(b, a, gpTs.ap(), gsTs.ap(), upTs.ap(), usTs.ap(),
                   dpTs.ap(), dsTs.ap(), ing.ap(), inu.ap(), ind.ap())
        decg4 = decg.ap().rearrange("(h p) c -> h p c", h=64)
        decu4 = decu.ap().rearrange("(h p) c -> h p c", h=64)
        decd4 = decd.ap().rearrange("(h p) c -> h p c", h=NH1)
        ph_ln(b, a, xin.ap())
        ph_qkv(b, a, w0qkv.ap(), bq0, qkvT.ap())
        ph_gather(b, ccsem, [(ing.ap().opt(), decg.ap().opt()),
                             (inu.ap().opt(), decu.ap().opt()),
                             (ind.ap().opt(), decd.ap().opt())])
        ph_attn(b, a, 0, qkvT.ap(), ab)
        ph_proj(b, a, w0proj.ap(), bp0, xin.ap(), x1.ap())
        ph_ln(b, a, x1.ap())
        ph_gateup0(b, a, wg.ap(), wu.ap(), sgb, sub, gu0.ap())
        ph_down0(b, a, wd.ap(), sdb, gu0.ap(), x1.ap(), x2.ap())
        ph_ln(b, a, x2.ap())
        ph_qkv(b, a, w1qkv.ap(), bq1, qkvT.ap())
        ph_attn(b, a, 1, qkvT.ap(), ab)
        ph_proj(b, a, w1proj.ap(), bp1, x2.ap(), x3.ap())
        ph_ln(b, a, x3.ap())
        for e in ENGINES:
            b.wait(e, ccsem, 3)
        ph_gateup1(b, a, decg4, decu4, gu1.ap())
        ph_down1(b, a, decd4, gu1.ap(), x3.ap(), xout.ap())
        b.finish()
    return nc



_NC = 8
FUSED = True


def kernel(**inputs):
    p = host_prep(inputs)
    consts = {k: p[k] for k in CONST_KEYS}
    x = np.asarray(inputs["L0_attn_x"], np.float32)
    xs = [np.ascontiguousarray(x[c].T) for c in range(_NC)]
    cores = list(range(_NC))

    def run(nc, shared):
        in_maps = [dict(shared, xin=xs[c]) for c in cores]
        res = run_bass_kernel_spmd(nc, in_maps, core_ids=cores)
        for c in cores:
            xs[c] = np.ascontiguousarray(np.asarray(res.results[c]["xout"]))

    if FUSED:
        shared = dict(
            consts, w0qkv=p["w0qkv"], b0qkv=p["b0qkv"], w0proj=p["w0proj"],
            b0proj=p["b0proj"], w1qkv=p["w1qkv"], b1qkv=p["b1qkv"],
            w1proj=p["w1proj"], b1proj=p["b1proj"], wg=p["wg0"], wu=p["wu0"],
            wd=p["wd0"], sg=p["sg0"], su=p["su0"], sd=p["sd0"])
        nc = build_fused()
        in_maps = []
        for c in cores:
            in_maps.append(dict(
                shared, xin=xs[c],
                gpTs=np.ascontiguousarray(p["gpT"][:, 2048 * c:2048 * (c + 1)]),
                upTs=np.ascontiguousarray(p["upT"][:, 2048 * c:2048 * (c + 1)]),
                gsTs=np.ascontiguousarray(p["gsT"][:, 2048 * c:2048 * (c + 1)]),
                usTs=np.ascontiguousarray(p["usT"][:, 2048 * c:2048 * (c + 1)]),
                dpTs=np.ascontiguousarray(p["dpT"][1024 * c:1024 * (c + 1)]),
                dsTs=np.ascontiguousarray(p["dsT"][64 * c:64 * (c + 1)])))
        res = run_bass_kernel_spmd(nc, in_maps, core_ids=cores)
        for c in cores:
            xs[c] = np.ascontiguousarray(np.asarray(res.results[c]["xout"]))
    else:
        run(build_attn_launch(0), dict(consts, wqkv=p["w0qkv"], bqkv=p["b0qkv"],
                                       wproj=p["w0proj"], bproj=p["b0proj"]))
        run(build_ffn0_launch(), dict(consts, wg=p["wg0"], wu=p["wu0"], wd=p["wd0"],
                                      sg=p["sg0"], su=p["su0"], sd=p["sd0"]))
        run(build_attn_launch(1), dict(consts, wqkv=p["w1qkv"], bqkv=p["b1qkv"],
                                       wproj=p["w1proj"], bproj=p["b1proj"]))
        run(build_ffn1_launch(), dict(consts, gpT=p["gpT"], upT=p["upT"], gsT=p["gsT"],
                                      usT=p["usT"], dpT=p["dpT"], dsT=p["dsT"]))

    return np.stack([xs[c].T for c in cores]).astype(np.float32)


# version 18
# speedup vs baseline: 1.0194x; 1.0194x over previous
import sys
sys.path.insert(0, '/opt/trn_rl_repo')
from contextlib import ExitStack
import math
import numpy as np
import ml_dtypes
import concourse.bass as bass
import concourse.mybir as mybir
from concourse.bass_utils import run_bass_kernel_spmd

F32 = mybir.dt.float32
BF16 = mybir.dt.bfloat16
I32 = mybir.dt.int32
AF = mybir.ActivationFunctionType
ALU = mybir.AluOpType
AX = mybir.AxisListType
BF = ml_dtypes.bfloat16

D = 2048
T = 2048
NKC = 16
NH = 16
HD = 128
NH0 = 44
H1 = 16384
NH1 = 128
EPS = 1e-5
NEG = -1.0e30

SCALE = {0: 1.0 / math.sqrt(HD), 1: 1.0 / HD}
SLOPES = [2.0 ** (-0.5 * (i + 1)) for i in range(NH)]

ENGINES = ("sync", "gpsimd", "tensor", "scalar", "vector")



def _block_in_out(w, nm):
    return np.ascontiguousarray(w.reshape(NKC, 128, nm, 128).transpose(2, 1, 0, 3))


def _block_out_in(w, nh):
    return np.ascontiguousarray(w.reshape(nh, 128, NKC, 128).transpose(0, 3, 2, 1))


def _block_down0(w):
    return np.ascontiguousarray(
        w.reshape(4, 512, NH0 // 2, 2, 128).transpose(0, 2, 4, 3, 1))


def _perm256(a):
    n = a.shape[0]
    return np.ascontiguousarray(
        a.reshape(n // 256, 128, 2, *a.shape[1:]).swapaxes(1, 2).reshape(a.shape))


def _colT(v):
    return np.ascontiguousarray(v.reshape(-1, 128).T)


def host_prep(inp):
    p = {}
    p["w0qkv"] = _block_in_out(np.asarray(inp["L0_attn_w_attn"], np.float32), 48)
    p["b0qkv"] = _colT(np.asarray(inp["L0_attn_b_attn"], np.float32))
    p["w0proj"] = _block_in_out(np.asarray(inp["L0_attn_w_proj"], np.float32), 16)
    p["b0proj"] = _colT(np.asarray(inp["L0_attn_b_proj"], np.float32))
    p["w1qkv"] = _block_in_out(np.asarray(inp["L1_attn_w_qkv"], np.float32), 48)
    p["b1qkv"] = _colT(np.asarray(inp["L1_attn_b_qkv"], np.float32))
    p["w1proj"] = _block_in_out(np.asarray(inp["L1_attn_w_proj"], np.float32), 16)
    p["b1proj"] = _colT(np.asarray(inp["L1_attn_b_proj"], np.float32))

    p["wg0"] = _block_out_in(np.asarray(inp["L0_ffn_wq_gate"], np.int32), NH0)
    p["wu0"] = _block_out_in(np.asarray(inp["L0_ffn_wq_up"], np.int32), NH0)
    p["wd0"] = _block_down0(np.asarray(inp["L0_ffn_wq_down"], np.int32))
    p["sg0"] = _colT(np.asarray(inp["L0_ffn_s_gate"], np.float32))
    p["su0"] = _colT(np.asarray(inp["L0_ffn_s_up"], np.float32))
    p["sd0"] = _colT(np.asarray(inp["L0_ffn_s_down"], np.float32))

    gp = _perm256(np.asarray(inp["L1_ffn_gate_W_packed"], np.int32))
    up = _perm256(np.asarray(inp["L1_ffn_up_W_packed"], np.int32))
    gs = _perm256(np.asarray(inp["L1_ffn_gate_scales"], np.float32))
    us = _perm256(np.asarray(inp["L1_ffn_up_scales"], np.float32))
    p["gpT"] = np.ascontiguousarray(gp.T)
    p["upT"] = np.ascontiguousarray(up.T)
    p["gsT"] = np.ascontiguousarray(gs.T)
    p["usT"] = np.ascontiguousarray(us.T)
    p["dpT"] = np.ascontiguousarray(np.asarray(inp["L1_ffn_down_W_packed"], np.int32).T)
    p["dsT"] = np.ascontiguousarray(np.asarray(inp["L1_ffn_down_scales"], np.float32).T)

    p["ident_bf"] = np.eye(128, dtype=BF)
    p["ident_f"] = np.eye(128, dtype=np.float32)
    p["ones_bf"] = np.ones((128, 1), dtype=BF)
    p["onesr_f"] = np.ones((1, 128), dtype=np.float32)
    jj, pp = np.meshgrid(np.arange(512), np.arange(128))
    p["relT"] = (pp - jj).astype(np.float32)
    p["relT2"] = np.concatenate([p["relT"], p["relT"] + 128.0], axis=1)
    c, r = np.meshgrid(np.arange(128), np.arange(128))
    p["masktriT"] = np.where(r > c, NEG, 0.0).astype(np.float32)
    return p



class Sem:
    def __init__(self, st, nc, name):
        self.h = st.enter_context(nc.semaphore(name))
        self.v = 0

    def inc(self, n=1):
        self.v += n
        return self.v


class B:
    def __init__(self, nc, st):
        self.nc = nc
        self.st = st
        self.ops = {e: [] for e in ENGINES}
        self.sQ = Sem(st, nc, "sQ")
        self.sG = Sem(st, nc, "sG")
        self.q = [Sem(st, nc, f"q{i}") for i in range(8)]
        self.g = [Sem(st, nc, f"g{i}") for i in range(4)]
        self.sPE = Sem(st, nc, "sPE")
        self.sAC = Sem(st, nc, "sAC")
        self.sDV = Sem(st, nc, "sDV")
        self.pb = Sem(st, nc, "pb")

    def sbuf(self, name, shape, dt):
        return self.st.enter_context(self.nc.sbuf_tensor(name, shape, dt))

    def psum(self, name, shape, dt):
        return self.st.enter_context(self.nc.psum_tensor(name, shape, dt))

    def em(self, eng, fn):
        self.ops[eng].append(fn)

    def wait(self, eng, sem, v):
        if v and v > 0:
            self.em(eng, lambda e, h=sem.h, vv=v: e.wait_ge(h, vv))

    def dma(self, eng, dst, src, sem):
        v = sem.inc(16)
        self.em(eng, lambda e, d=dst, s=src, h=sem.h: e.dma_start(d, s).then_inc(h, 16))
        return v

    def op(self, eng, fn, sem=None, k=1):
        if eng == "vector" or eng == "scalar":
            s = self.sDV if eng == "vector" else self.sAC
            assert sem is None or sem is s
            if s.v > 0:
                self.em(eng, lambda e, h=s.h, vv=s.v: e.wait_ge(h, vv))
            v = s.inc(1)
            self.em(eng, lambda e, f=fn, h=s.h: f(e).then_inc(h, 1))
            return v
        if sem is None:
            self.em(eng, fn)
            return None
        v = sem.inc(k)
        self.em(eng, lambda e, f=fn, h=sem.h, kk=k: f(e).then_inc(h, kk))
        return v

    def barrier(self):
        sems = [self.sQ, self.sG, self.sPE, self.sAC, self.sDV] + self.q + self.g
        vals = [s.v for s in sems]
        target = self.pb.v + len(ENGINES)
        self.pb.v = target
        for eng in ENGINES:
            def _bar(e, sems=sems, vals=vals, pbh=self.pb.h, tgt=target):
                for s, v in zip(sems, vals):
                    if v > 0:
                        e.wait_ge(s.h, v)
                e.nop().then_inc(pbh, 1)
                e.wait_ge(pbh, tgt)
            self.em(eng, _bar)

    def finish(self):
        block = self.st.enter_context(self.nc.Block())
        for eng in ENGINES:
            body = getattr(block, eng)
            lst = self.ops[eng]

            def _run(e, lst=lst):
                for fn in lst:
                    fn(e)
            body(_run)


class Arena:

    def __init__(self, b: B):
        self.big = b.sbuf("big", [128, NKC * T], BF16)
        self.stb = b.sbuf("stb", [128, 8192], F32)
        self.ptb = b.sbuf("ptb", [128, 8192], BF16)
        self.xs = [b.sbuf(f"xs{i}", [128, T], F32) for i in range(2)]
        self.osg = [b.sbuf(f"osg{i}", [128, T], F32) for i in range(2)]
        self.qb = [b.sbuf(f"qbuf{i}", [128, T], BF16) for i in range(2)]
        self.wst2 = [b.sbuf(f"wst2{i}", [128, T], BF16) for i in range(2)]
        self.gs = [b.sbuf(f"gsb{i}", [128, 512], F32) for i in range(2)]
        self.stats = b.sbuf("stats", [1, 4 * T], F32)
        self.tmpf = b.sbuf("tmpf", [128, 512], F32)
        self.small = b.sbuf("small", [128, 40], F32)
        self.ps01 = b.psum("ps01", [128, 1024], F32)
        self.ps = [self.ps01[:, 0:512], self.ps01[:, 512:1024]] + \
            [b.psum(f"ps{i}", [128, 512], F32) for i in range(2, 8)]
        self.ident_bf = b.sbuf("identbf", [128, 128], BF16)
        self.ident_f = b.sbuf("identf", [128, 128], F32)
        self.ones_bf = b.sbuf("onesbf", [128, 1], BF16)
        self.onesr_f = b.sbuf("onesrf", [1, 128], F32)
        self.relT = b.sbuf("relTb", [128, 512], F32)
        self.relT2 = b.sbuf("relT2b", [128, 1024], F32)
        self.mtri = b.sbuf("mtrib", [128, 128], F32)


def load_consts(b: B, a: Arena, dt):
    v = 0
    for nm, buf in (("ident_bf", a.ident_bf), ("ident_f", a.ident_f),
                    ("ones_bf", a.ones_bf), ("onesr_f", a.onesr_f),
                    ("relT", a.relT), ("relT2", a.relT2), ("masktriT", a.mtri)):
        v = b.dma("sync", buf[:, :], dt[nm].ap(), b.q[6])
    return v



def ph_ln(b: B, a: Arena, xdram):
    sq = a.qb[0]
    xld = [b.dma("sync", a.xs[kc][:, :], xdram[kc * 128:(kc + 1) * 128, :], b.q[kc])
           for kc in range(2)]
    pe_kc = [0] * NKC
    for kc in range(NKC):
        b.wait("scalar", b.q[kc % 2], xld[kc])
        if kc >= 1:
            b.wait("scalar", b.sPE, pe_kc[kc - 1])
        bigc = a.big[:, kc * T:(kc + 1) * T]
        b.op("scalar", lambda e, o=bigc, i=a.xs[kc % 2]: e.activation(
            o, i[:, :], AF.Identity))
        v = b.op("scalar", lambda e, i=a.xs[kc % 2]: e.activation(
            sq[:, :], i[:, :], AF.Square), b.sAC)
        if kc + 2 < NKC:
            b.wait("sync", b.sAC, v)
            xld.append(b.dma("sync", a.xs[kc % 2][:, :],
                             xdram[(kc + 2) * 128:(kc + 3) * 128, :], b.q[kc % 2]))
        b.wait("tensor", b.sAC, v)
        for w in range(4):
            b.op("tensor", lambda e, o=bigc, w_=w, s=(kc == 0), t=(kc == NKC - 1):
                 e.matmul(a.ps[w_][0:1, :], a.ones_bf[:, :],
                          o[:, w_ * 512:(w_ + 1) * 512], start=s, stop=t))
            pe_kc[kc] = b.op("tensor", lambda e, w_=w, s=(kc == 0), t=(kc == NKC - 1):
                             e.matmul(a.ps[4 + w_][0:1, :], a.ones_bf[:, :],
                                      sq[:, w_ * 512:(w_ + 1) * 512], start=s, stop=t),
                             b.sPE)
    b.wait("vector", b.sPE, pe_kc[NKC - 1])
    for w in range(4):
        b.op("vector", lambda e, w_=w: e.tensor_copy(
            a.stats[0:1, w_ * 512:(w_ + 1) * 512], a.ps[w_][0:1, :]))
        b.op("vector", lambda e, w_=w: e.tensor_copy(
            a.stats[0:1, T + w_ * 512:T + (w_ + 1) * 512], a.ps[4 + w_][0:1, :]))
    b.op("vector", lambda e: e.tensor_scalar(a.stats[0:1, 0:T], a.stats[0:1, 0:T],
                                             1.0 / D, None, ALU.mult))
    b.op("vector", lambda e: e.tensor_scalar(a.stats[0:1, T:2 * T], a.stats[0:1, T:2 * T],
                                             1.0 / D, None, ALU.mult))
    b.op("vector", lambda e: e.tensor_tensor(a.stats[0:1, 2 * T:3 * T], a.stats[0:1, 0:T],
                                             a.stats[0:1, 0:T], ALU.mult))
    b.op("vector", lambda e: e.tensor_tensor(a.stats[0:1, T:2 * T], a.stats[0:1, T:2 * T],
                                             a.stats[0:1, 2 * T:3 * T], ALU.subtract))
    dv = b.op("vector", lambda e: e.tensor_scalar(a.stats[0:1, T:2 * T],
                                                  a.stats[0:1, T:2 * T], EPS, None,
                                                  ALU.add), b.sDV)
    b.wait("scalar", b.sDV, dv)
    av = b.op("scalar", lambda e: e.activation(a.stats[0:1, 2 * T:3 * T], a.stats[0:1, T:2 * T],
                                               AF.Sqrt), b.sAC)
    b.wait("vector", b.sAC, av)
    b.op("vector", lambda e: e.reciprocal(a.stats[0:1, 2 * T:3 * T], a.stats[0:1, 2 * T:3 * T]))
    dv = b.op("vector", lambda e: e.tensor_tensor(a.stats[0:1, 3 * T:4 * T], a.stats[0:1, 0:T],
                                                  a.stats[0:1, 2 * T:3 * T], ALU.mult), b.sDV)
    b.wait("tensor", b.sDV, dv)
    pv = None
    for w in range(4):
        b.op("tensor", lambda e, w_=w: e.matmul(
            a.ps[w_][:, :], a.onesr_f[:, :], a.stats[0:1, 2 * T + w_ * 512:2 * T + (w_ + 1) * 512],
            start=True, stop=True))
        pv = b.op("tensor", lambda e, w_=w: e.matmul(
            a.ps[4 + w_][:, :], a.onesr_f[:, :], a.stats[0:1, 3 * T + w_ * 512:3 * T + (w_ + 1) * 512],
            start=True, stop=True), b.sPE if w == 3 else None)
    b.wait("vector", b.sPE, pv)
    for w in range(4):
        b.op("vector", lambda e, w_=w: e.tensor_copy(
            a.stb[:, w_ * 512:(w_ + 1) * 512], a.ps[w_][:, :]))
        b.op("vector", lambda e, w_=w: e.tensor_copy(
            a.stb[:, 2048 + w_ * 512:2048 + (w_ + 1) * 512], a.ps[4 + w_][:, :]))
    tmp2 = a.stb[:, 4096:6144]
    for kc in range(NKC):
        sl = slice(kc * T, (kc + 1) * T)
        b.op("vector", lambda e, s=sl: e.tensor_tensor(
            tmp2, a.big[:, s], a.stb[:, 0:2048], ALU.mult))
        b.op("vector", lambda e, s=sl: e.tensor_tensor(
            a.big[:, s], tmp2, a.stb[:, 2048:4096], ALU.subtract))
    b.barrier()


def ph_qkv(b: B, a: Arena, wdram, bcol, qkvT):
    wstrip = a.wst2
    mm_done = [0] * 48
    ev_done = [0] * 48
    st_done = [0] * 48
    for m in range(48):
        if m >= 2:
            b.wait("gpsimd", b.sPE, mm_done[m - 2])
        wld = b.dma("gpsimd",
                    wstrip[m % 2][:, :].rearrange("p (k c) -> p k c", k=NKC),
                    wdram[m], b.g[m % 2])
        b.wait("tensor", b.g[m % 2], wld)
        if m >= 2:
            b.wait("tensor", b.sAC, ev_done[m - 2])
        base = 0 if m % 2 == 0 else 4
        v = None
        for kc in range(NKC):
            for w in range(4):
                v = b.op("tensor", lambda e, w_=w, kc_=kc, m_=m, ba=base: e.matmul(
                    a.ps[ba + w_][:, :],
                    wstrip[m_ % 2][:, kc_ * 128:(kc_ + 1) * 128],
                    a.big[:, kc_ * T + w_ * 512: kc_ * T + (w_ + 1) * 512],
                    start=(kc_ == 0), stop=(kc_ == NKC - 1)),
                    b.sPE if kc == NKC - 1 and w == 3 else None)
        mm_done[m] = v
        b.wait("scalar", b.sPE, v)
        if m >= 2:
            b.wait("scalar", b.q[4 + m % 2], st_done[m - 2])
        for w in range(4):
            v = b.op("scalar", lambda e, w_=w, m_=m, ba=base: e.activation(
                a.qb[m_ % 2][:, w_ * 512:(w_ + 1) * 512], a.ps[ba + w_][:, :],
                AF.Identity, bias=bcol[:, m_:m_ + 1]),
                b.sAC if w == 3 else None)
        ev_done[m] = v
        b.wait("sync", b.sAC, v)
        st_done[m] = b.dma("sync", qkvT[m, :, :], a.qb[m % 2][:, :], b.q[4 + m % 2])
    b.barrier()


def ph_attn(b: B, a: Arena, layer, qkvT, ab):
    qkvh, vtm, rlb, rlrow, gsc, aph = ab
    s = SCALE[layer]
    trps = a.ps[2]
    PS_L, PS_Y, PS_RL = 3, 4, 5
    cur_ld = [b.dma("sync", qkvh[0][t], qkvT[16 * t, :, :], b.q[0]) for t in range(3)]
    head_pe_done = 0
    consumed = {0: (b.sPE, 0), 1: (b.sPE, 0)}
    final_tt = 0
    vtm_copy = 0
    for h in range(NH):
        cur = qkvh[h % 2]
        nxt = qkvh[(h + 1) % 2]
        my_ld = cur_ld
        if h + 1 < NH:
            if h >= 1:
                b.wait("sync", b.sPE, head_pe_done)
            cur_ld = [b.dma("sync", nxt[t], qkvT[16 * t + h + 1, :, :], b.q[(h + 1) % 2])
                      for t in range(3)]
        qh, kh, vh = cur[0], cur[1], cur[2]
        b.wait("tensor", b.q[h % 2], my_ld[2])
        b.wait("vector", b.q[h % 2], my_ld[2])
        for g4 in range(4):
            if h > 0 and g4 == 0:
                b.wait("tensor", b.sDV, final_tt)
            cv = b.op("vector", lambda e, g4_=g4, vh_=vh: e.tensor_copy(
                a.tmpf[:, :], vh_[:, g4_ * 512:(g4_ + 1) * 512]), b.sDV)
            b.wait("tensor", b.sDV, cv)
            if g4 > 0:
                b.wait("tensor", b.sDV, vtm_copy)
            v = None
            for i4 in range(4):
                v = b.op("tensor", lambda e, i4_=i4: e.transpose(
                    trps[:, i4_ * 128:(i4_ + 1) * 128],
                    a.tmpf[:, i4_ * 128:(i4_ + 1) * 128], a.ident_f[:, :]),
                    b.sPE if i4 == 3 else None)
            b.wait("vector", b.sPE, v)
            vtm_copy = b.op("vector", lambda e, g4_=g4: e.tensor_copy(
                vtm[:, g4_ * 512:(g4_ + 1) * 512], trps[:, :]), b.sDV)
        if layer == 1:
            b.op("vector", lambda e, sl=SLOPES[h] / s: e.tensor_scalar(
                aph[:, :], a.relT2[:, :], sl, None, ALU.mult))
        for qb in range(4):
            nk = 4 * qb + 4
            expv = None
            exps = [None] * nk

            def emit_ly(kc, nk=nk):
                b.wait("tensor", b.sAC, exps[kc])
                if kc == 0:
                    b.wait("tensor", b.sDV, final_tt)
                pt = a.ptb[:, kc * 512:(kc + 1) * 512]
                b.op("tensor", lambda e, pt_=pt, s_=(kc == 0), t_=(kc == nk - 1):
                     e.matmul(a.ps[PS_L][0:1, :], a.ones_bf[:, :], pt_,
                              start=s_, stop=t_))
                return b.op("tensor", lambda e, pt_=pt, kc_=kc, s_=(kc == 0),
                            t_=(kc == nk - 1): e.matmul(
                                a.ps[PS_Y][:, :], vtm[:, kc_ * 128:(kc_ + 1) * 128],
                                pt_, start=s_, stop=t_),
                            b.sPE if kc == nk - 1 else None)

            for kc in range(nk):
                par = kc % 2
                psb = a.ps[par]
                csem, cval = consumed[par]
                b.wait("tensor", csem, cval)
                v = b.op("tensor", lambda e, ps=psb, kc_=kc, qb_=qb, kh_=kh, qh_=qh:
                         e.matmul(ps[:, :], kh_[:, kc_ * 128:(kc_ + 1) * 128],
                                  qh_[:, qb_ * 512:(qb_ + 1) * 512],
                                  start=True, stop=True), b.sPE)
                diag = kc >= 4 * qb
                off = 128 * (kc - 4 * qb) if diag else 0
                if layer == 1:
                    b.wait("vector", b.sPE, v)
                    st = a.stb[:, kc * 512:(kc + 1) * 512]
                    Cs = SLOPES[h] * (128 * kc - 512 * qb) / s
                    mk = b.op("vector", lambda e, st_=st, ps=psb, C_=Cs:
                              e.scalar_tensor_tensor(
                                  st_, ps[:, :], C_, aph[:, 0:512],
                                  ALU.add, ALU.add), b.sDV)
                    consumed[par] = (b.sDV, mk)
                    if diag:
                        if off > 0:
                            b.op("vector", lambda e, st_=st, o=off:
                                 e.memset(st_[:, 0:o], NEG))
                        mk = b.op("vector", lambda e, st_=st, o=off:
                                  e.tensor_tensor(
                                      st_[:, o:o + 128], st_[:, o:o + 128],
                                      a.mtri[:, :], ALU.add), b.sDV)
                    b.wait("scalar", b.sDV, mk)
                    expv = b.op("scalar", lambda e, kc_=kc, st_=st: e.activation(
                        a.ptb[:, kc_ * 512:(kc_ + 1) * 512], st_, AF.Exp, scale=s),
                        b.sAC)
                    exps[kc] = expv
                else:
                    if diag:
                        b.wait("vector", b.sPE, v)
                        if off > 0:
                            b.op("vector", lambda e, ps=psb, o=off:
                                 e.memset(ps[:, 0:o], NEG))
                        mk = b.op("vector", lambda e, ps=psb, o=off: e.tensor_tensor(
                            ps[:, o:o + 128], ps[:, o:o + 128], a.mtri[:, :],
                            ALU.add), b.sDV)
                        b.wait("scalar", b.sDV, mk)
                    b.wait("scalar", b.sPE, v)
                    expv = b.op("scalar", lambda e, kc_=kc, ps=psb: e.activation(
                        a.ptb[:, kc_ * 512:(kc_ + 1) * 512], ps[:, :], AF.Exp,
                        scale=s), b.sAC)
                    consumed[par] = (b.sAC, expv)
                    exps[kc] = expv
                if kc >= 2:
                    emit_ly(kc - 2)
            emit_ly(nk - 2)
            v = emit_ly(nk - 1)
            b.wait("vector", b.sPE, v)
            v = b.op("vector", lambda e: e.reciprocal(rlrow[:, :], a.ps[PS_L][0:1, :]),
                     b.sDV)
            b.wait("tensor", b.sDV, v)
            v = b.op("tensor", lambda e: e.matmul(
                a.ps[PS_RL][:, :], a.onesr_f[:, :], rlrow[:, :],
                start=True, stop=True), b.sPE)
            b.wait("vector", b.sPE, v)
            b.op("vector", lambda e: e.tensor_copy(rlb[:, :], a.ps[PS_RL][:, :]))
            final_tt = b.op("vector", lambda e, h_=h, qb_=qb: e.tensor_tensor(
                a.big[:, h_ * T + qb_ * 512: h_ * T + (qb_ + 1) * 512],
                a.ps[PS_Y][:, :], rlb[:, :], ALU.mult), b.sDV)
        head_pe_done = b.sPE.v
    b.barrier()


def ph_proj(b: B, a: Arena, wdram, bcol, xdram, xout):
    wstrip = a.wst2
    xld = [0] * 16
    for m in range(2):
        xld[m] = b.dma("sync", a.xs[m][:, :], xdram[m * 128:(m + 1) * 128, :], b.q[m])
    mm_done = [0] * 16
    ev_done = [0] * 16
    dv_done = [0] * 16
    st_done = [0] * 16
    for m in range(16):
        if m >= 2:
            b.wait("gpsimd", b.sPE, mm_done[m - 2])
        wld = b.dma("gpsimd",
                    wstrip[m % 2][:, :].rearrange("p (k c) -> p k c", k=NKC),
                    wdram[m], b.g[m % 2])
        b.wait("tensor", b.g[m % 2], wld)
        if m >= 2:
            b.wait("tensor", b.sAC, ev_done[m - 2])
        base = 0 if m % 2 == 0 else 4
        v = None
        for kc in range(NKC):
            for w in range(4):
                v = b.op("tensor", lambda e, w_=w, kc_=kc, m_=m, ba=base: e.matmul(
                    a.ps[ba + w_][:, :],
                    wstrip[m_ % 2][:, kc_ * 128:(kc_ + 1) * 128],
                    a.big[:, kc_ * T + w_ * 512: kc_ * T + (w_ + 1) * 512],
                    start=(kc_ == 0), stop=(kc_ == NKC - 1)),
                    b.sPE if kc == NKC - 1 and w == 3 else None)
        mm_done[m] = v
        b.wait("scalar", b.sPE, v)
        if m >= 2:
            b.wait("scalar", b.q[4 + m % 2], st_done[m - 2])
        for w in range(4):
            v = b.op("scalar", lambda e, w_=w, m_=m, ba=base: e.activation(
                a.osg[m_ % 2][:, w_ * 512:(w_ + 1) * 512], a.ps[ba + w_][:, :],
                AF.Identity, bias=bcol[:, m_:m_ + 1]),
                b.sAC if w == 3 else None)
        ev_done[m] = v
        b.wait("vector", b.sAC, v)
        b.wait("vector", b.q[m % 2], xld[m])
        dv_done[m] = b.op("vector", lambda e, m_=m: e.tensor_tensor(
            a.osg[m_ % 2][:, :], a.osg[m_ % 2][:, :], a.xs[m_ % 2][:, :], ALU.add),
            b.sDV)
        if m + 2 < 16:
            b.wait("sync", b.sDV, dv_done[m])
            xld[m + 2] = b.dma("sync", a.xs[m % 2][:, :],
                               xdram[(m + 2) * 128:(m + 3) * 128, :], b.q[m % 2])
        b.wait("sync", b.sDV, dv_done[m])
        st_done[m] = b.dma("sync", xout[m * 128:(m + 1) * 128, :],
                           a.osg[m % 2][:, :], b.q[4 + m % 2])
    b.barrier()


def ph_gateup0(b: B, a: Arena, wg, wu, sgcol, sucol, gudram):
    wst = [a.stb[:, 0:2048].bitcast(I32), a.stb[:, 2048:4096].bitcast(I32)]
    wstu = [a.stb[:, 4096:6144].bitcast(I32), a.stb[:, 6144:8192].bitcast(I32)]
    wbg = [a.ptb[:, 0:2048], a.ptb[:, 2048:4096]]
    wbu = [a.ptb[:, 4096:6144], a.ptb[:, 6144:8192]]
    guo = a.qb
    mm_half = {}
    dv_half = {}
    st_done = [0] * NH0
    deq_done = [0] * NH0
    for h in range(NH0):
        if h >= 2:
            b.wait("sync", b.sDV, deq_done[h - 2])
        b.dma("sync", wst[h % 2].rearrange("p (k c) -> p k c", k=NKC),
              wg[h], b.q[h % 2])
        l2 = b.dma("sync", wstu[h % 2].rearrange("p (k c) -> p k c", k=NKC),
                   wu[h], b.q[h % 2])
        b.wait("vector", b.q[h % 2], l2)
        if h >= 2:
            b.wait("vector", b.sPE, mm_half.get((h - 2, 1), 0))
        b.op("vector", lambda e, h_=h: e.tensor_scalar(
            wbg[h_ % 2], wst[h_ % 2], -128, None, ALU.add))
        deq_done[h] = b.op("vector", lambda e, h_=h: e.tensor_scalar(
            wbu[h_ % 2], wstu[h_ % 2], -128, None, ALU.add), b.sDV)
        b.wait("tensor", b.sDV, deq_done[h])
        for half in range(2):
            base = 4 * half
            if h >= 1:
                b.wait("tensor", b.sDV, dv_half.get((h - 1, half), 0))
            v = None
            for kc in range(NKC):
                for w in range(2):
                    tw = 2 * half + w
                    b.op("tensor", lambda e, h_=h, kc_=kc, w_=w, tw_=tw, ba=base:
                         e.matmul(a.ps[ba + w_][:, :],
                                  wbg[h_ % 2][:, kc_ * 128:(kc_ + 1) * 128],
                                  a.big[:, kc_ * T + tw_ * 512: kc_ * T + (tw_ + 1) * 512],
                                  start=(kc_ == 0), stop=(kc_ == NKC - 1)))
                for w in range(2):
                    tw = 2 * half + w
                    v = b.op("tensor", lambda e, h_=h, kc_=kc, w_=w, tw_=tw, ba=base:
                             e.matmul(a.ps[ba + 2 + w_][:, :],
                                      wbu[h_ % 2][:, kc_ * 128:(kc_ + 1) * 128],
                                      a.big[:, kc_ * T + tw_ * 512: kc_ * T + (tw_ + 1) * 512],
                                      start=(kc_ == 0), stop=(kc_ == NKC - 1)),
                             b.sPE if kc == NKC - 1 and w == 1 else None)
            mm_half[(h, half)] = v
            b.wait("scalar", b.sPE, v)
            prev_dv = dv_half.get((h, half - 1), 0) if half == 1 \
                else dv_half.get((h - 1, 1), 0)
            b.wait("scalar", b.sDV, prev_dv)
            v = None
            for w in range(2):
                v = b.op("scalar", lambda e, h_=h, w_=w, ba=base: e.activation(
                    a.gs[w_][:, :], a.ps[ba + w_][:, :], AF.Silu,
                    scale=sgcol[:, h_:h_ + 1]), b.sAC if w == 1 else None)
            b.wait("vector", b.sAC, v)
            if h >= 2 and half == 0:
                b.wait("vector", b.g[h % 2], st_done[h - 2])
            for w in range(2):
                tw = 2 * half + w
                v = b.op("vector", lambda e, h_=h, w_=w, tw_=tw, ba=base:
                         e.scalar_tensor_tensor(
                             guo[h_ % 2][:, tw_ * 512:(tw_ + 1) * 512],
                             a.ps[ba + 2 + w_][:, :], sucol[:, h_:h_ + 1],
                             a.gs[w_][:, :], ALU.mult, ALU.mult),
                         b.sDV if w == 1 else None)
            dv_half[(h, half)] = v
        b.wait("gpsimd", b.sDV, v)
        st_done[h] = b.dma("gpsimd", gudram[h, :, :], guo[h % 2][:, :], b.g[h % 2])
    b.barrier()


def _down_common(b, a, nh, wpair_fn, gudram, xdram, xout, sdcol):
    gub = [a.ptb[:, 0:2048], a.ptb[:, 2048:4096]]
    st_prev = {}
    pg = {0: 0, 1: 0}
    nhp = nh // 2
    gpair = 0
    for w2 in range(2):
        for dg in range(4):
            last_pe = 0
            grp_start_ac = b.sAC.v
            for hp in range(nhp):
                par = gpair % 2
                gpair += 1
                geng = "sync" if par == 0 else "gpsimd"
                gsem = b.q[2] if par == 0 else b.g[2]
                b.wait(geng, b.sPE, pg[par])
                wsem, wv, wpair = wpair_fn(hp, dg, par, pg[par])
                guld = b.dma(geng,
                             gub[par].rearrange("p (h c) -> p h c", h=2),
                             gudram[2 * hp:2 * hp + 2, :,
                                    w2 * 1024:(w2 + 1) * 1024]
                             .rearrange("h p c -> p h c"), gsem)
                b.wait("tensor", wsem, wv)
                b.wait("tensor", gsem, guld)
                if hp == 0:
                    b.wait("tensor", b.sAC, grp_start_ac)
                v = None
                for hi in range(2):
                    h = 2 * hp + hi
                    for d in range(4):
                        for c in range(2):
                            v = b.op("tensor", lambda e, wp=wpair, hi_=hi, h_=h,
                                     d_=d, c_=c, par_=par: e.matmul(
                                         a.ps[2 * d_ + c_][:, :],
                                         wp[:, hi_ * 512 + d_ * 128:
                                            hi_ * 512 + (d_ + 1) * 128],
                                         gub[par_][:, hi_ * 1024 + c_ * 512:
                                                   hi_ * 1024 + (c_ + 1) * 512],
                                         start=(h_ == 0), stop=(h_ == nh - 1)),
                                     b.sPE if hi == 1 and d == 3 and c == 1 else None)
                pg[par] = v
                last_pe = v
            b.wait("scalar", b.sPE, last_pe)
            for d in range(4):
                dout = 4 * dg + d
                b.wait("sync", b.sDV, st_prev.get(("x", d % 2), 0))
                xv = b.dma("sync", a.xs[d % 2][:, 0:1024],
                           xdram[dout * 128:(dout + 1) * 128, w2 * 1024:(w2 + 1) * 1024],
                           b.q[4 + d % 2])
                b.wait("scalar", b.g[d % 2], st_prev.get(("o", d % 2), 0))
                v = None
                for c in range(2):
                    kw = {}
                    if sdcol is not None:
                        kw["scale"] = sdcol[:, dout:dout + 1]
                    v = b.op("scalar", lambda e, d_=d, c_=c, kw_=kw: e.activation(
                        a.osg[d_ % 2][:, c_ * 512:(c_ + 1) * 512],
                        a.ps[2 * d_ + c_][:, :], AF.Identity, **kw_),
                        b.sAC if c == 1 else None)
                b.wait("vector", b.sAC, v)
                b.wait("vector", b.q[4 + d % 2], xv)
                dv = b.op("vector", lambda e, d_=d: e.tensor_tensor(
                    a.osg[d_ % 2][:, 0:1024], a.osg[d_ % 2][:, 0:1024],
                    a.xs[d_ % 2][:, 0:1024], ALU.add), b.sDV)
                st_prev[("x", d % 2)] = dv
                b.wait("gpsimd", b.sDV, dv)
                st_prev[("o", d % 2)] = b.dma(
                    "gpsimd", xout[dout * 128:(dout + 1) * 128, w2 * 1024:(w2 + 1) * 1024],
                    a.osg[d % 2][:, 0:1024], b.g[d % 2])
    b.barrier()


def ph_down0(b: B, a: Arena, wd, sdcol, gudram, xdram, xout):
    wst = [a.stb[:, 0:1024].bitcast(I32), a.stb[:, 1024:2048].bitcast(I32)]
    wbb = [a.ptb[:, 4096:5120], a.ptb[:, 5120:6144]]
    deq_h = {0: 0, 1: 0}

    def wpair(hp, dg, par, pe_guard):
        b.wait("scalar", b.sDV, deq_h[par])
        b.dma("scalar", wst[par].rearrange("p (h c) -> p h c", h=2),
              wd[dg, hp, :, :, :], b.q[6 + par])
        lv = b.q[6 + par].v
        b.wait("vector", b.q[6 + par], lv)
        b.wait("vector", b.sPE, pe_guard)
        deq_h[par] = b.op("vector", lambda e, p_=par: e.tensor_scalar(
            wbb[p_], wst[p_], -128, None, ALU.add), b.sDV)
        return b.sDV, deq_h[par], wbb[par]

    _down_common(b, a, NH0, wpair, gudram, xdram, xout, sdcol)


def ph_down1(b: B, a: Arena, decd, gudram, xdram, xout):
    wbb = [a.ptb[:, 4096:5120], a.ptb[:, 5120:6144]]

    def wpair(hp, dg, par, pe_guard):
        b.wait("scalar", b.sPE, pe_guard)
        v = b.dma("scalar", wbb[par].rearrange("p (h c) -> p h c", h=2),
                  decd[2 * hp:2 * hp + 2, :, 512 * dg:512 * (dg + 1)]
                  .rearrange("h p c -> p h c"), b.q[6 + par])
        return b.q[6 + par], v, wbb[par]

    _down_common(b, a, NH1, wpair, gudram, xdram, xout, None)


def ph_decode1(b: B, a: Arena, gpTs, gsTs, upTs, usTs, dpTs, dsTs,
               ing, inu, ind):
    pk = [a.xs[0][:, :].bitcast(I32), a.xs[1][:, :].bitcast(I32)]
    sct = [a.osg[0][:, :], a.osg[1][:, :]]
    t0 = a.stb[:, 0:2048].bitcast(I32)
    t3 = a.stb[:, 2048:4096].bitcast(I32)
    t1 = a.stb[:, 4096:6144].bitcast(I32)
    t2 = a.stb[:, 6144:8192].bitcast(I32)
    bigf = a.big[:, :].bitcast(F32)
    cf = bigf[:, 0:2048]
    sg = bigf[:, 2048:4096]
    v0 = bigf[:, 4096:6144]
    wo = [a.ptb[:, i * 2048:(i + 1) * 2048] for i in range(4)]
    last_par = {0: 0, 1: 0}
    prev_vmul = [0]

    def decode_unit(packed_ap, scale_ap, out_lo, out_hi, parity):
        b.wait("sync", b.sDV, last_par[parity])
        b.dma("sync", pk[parity], packed_ap, b.q[parity])
        ls = b.dma("sync", sct[parity], scale_ap, b.q[parity])
        b.wait("vector", b.q[parity], ls)
        b.wait("vector", b.g[parity], b.g[parity].v)
        for hi in range(2):
            wbuf = wo[2 * hi + parity]
            if hi == 0:
                b.op("vector", lambda e, p_=parity: e.tensor_scalar(
                    t0, pk[p_], 7, 22, ALU.bitwise_and, ALU.logical_shift_left))
                mka = b.op("vector", lambda e, p_=parity: e.tensor_scalar(
                    t3, pk[p_], 8, None, ALU.bitwise_and), b.sDV)
                sgm = -0.25
            else:
                b.op("vector", lambda e, p_=parity: e.tensor_scalar(
                    t0, pk[p_], 0x70, 18, ALU.bitwise_and, ALU.logical_shift_left))
                mka = b.op("vector", lambda e, p_=parity: e.tensor_scalar(
                    t3, pk[p_], 128, None, ALU.bitwise_and), b.sDV)
                sgm = -0.25 / 16.0
            b.wait("scalar", b.sDV, mka)
            b.wait("scalar", b.sDV, prev_vmul[0])
            b.op("scalar", lambda e, sgm_=sgm: e.activation(
                sg, t3, AF.Copy, scale=sgm_, bias=1.0))
            b.op("vector", lambda e: e.tensor_scalar(
                t1, t0, 0x3F000000, None, ALU.add))
            b.op("vector", lambda e: e.tensor_scalar(
                t2, t0, 2 << 22, None, ALU.min))
            mkb = b.op("vector", lambda e: e.tensor_scalar(
                t2, t2, -1, 2 << 22, ALU.mult, ALU.add), b.sDV)
            b.wait("scalar", b.sDV, mkb)
            cv = b.op("scalar", lambda e: e.activation(
                cf, t2, AF.Copy, scale=float(0.25 / (1 << 22))), b.sAC)
            b.wait("vector", b.sAC, cv)
            b.op("vector", lambda e: e.tensor_tensor(
                v0, t1.bitcast(F32), cf, ALU.subtract))
            b.op("vector", lambda e: e.tensor_tensor(v0, v0, sg, ALU.mult))
            stv = b.op("vector", lambda e, wb=wbuf, p_=parity: e.tensor_tensor(
                wb, v0, sct[p_], ALU.mult), b.sDV)
            prev_vmul[0] = stv
            b.wait("gpsimd", b.sDV, stv)
            for dst, src_fn in (out_lo if hi == 0 else out_hi):
                b.dma("gpsimd", dst, src_fn(wbuf), b.g[parity])
        last_par[parity] = prev_vmul[0]

    for pT, sT, slab in ((gpTs, gsTs, ing), (upTs, usTs, inu)):
        slab5 = slab.rearrange("(h r two) (k c) -> two r h k c",
                               h=8, two=2, k=NKC)
        for jj in range(8):
            packed = pT[jj * 128:(jj + 1) * 128, :]
            scl = sT[jj * 8:(jj + 1) * 8, :].unsqueeze(1).broadcast_to([8, 16, 2048])
            out_lo = [
                (slab5[0, :, :, 2 * jj, :],
                 lambda w: w[0:64, :].rearrange("p (h c) -> p h c", h=8)),
                (slab5[0, :, :, 2 * jj + 1, :],
                 lambda w: w[64:128, :].rearrange("p (h c) -> p h c", h=8)),
            ]
            out_hi = [
                (slab5[1, :, :, 2 * jj, :],
                 lambda w: w[0:64, :].rearrange("p (h c) -> p h c", h=8)),
                (slab5[1, :, :, 2 * jj + 1, :],
                 lambda w: w[64:128, :].rearrange("p (h c) -> p h c", h=8)),
            ]
            decode_unit(packed, scl, out_lo, out_hi, jj % 2)
    for jj in range(8):
        packed = dpTs[jj * 128:(jj + 1) * 128, :]
        scl = dsTs[jj * 8:(jj + 1) * 8, :].unsqueeze(1).broadcast_to([8, 16, 2048])
        out_lo = [(ind[256 * jj:256 * jj + 128, :], lambda w: w)]
        out_hi = [(ind[256 * jj + 128:256 * jj + 256, :], lambda w: w)]
        decode_unit(packed, scl, out_lo, out_hi, jj % 2)
    b.wait("gpsimd", b.g[0], b.g[0].v)
    b.wait("gpsimd", b.g[1], b.g[1].v)
    b.barrier()


def ph_gather(b: B, ccsem, pairs):
    for src_ap, dst_ap in pairs:
        b.op("gpsimd", lambda e, s=src_ap, d=dst_ap: e.collective_compute(
            "AllGather", ALU.bypass, replica_groups=[list(range(8))],
            ins=[s], outs=[d]), ccsem)
    b.barrier()


def ph_gateup1(b: B, a: Arena, decg, decu, gudram):
    wgs = [a.stb[:, 0:2048].bitcast(BF16), a.stb[:, 2048:4096].bitcast(BF16)]
    wus = [a.stb[:, 4096:6144].bitcast(BF16), a.stb[:, 6144:8192].bitcast(BF16)]
    guo = a.qb
    mm_last = {}
    dv_half = {}
    st_done = {}
    for hp in range(64):
        eng = "sync" if hp % 2 == 0 else "gpsimd"
        wsem = b.q[0] if hp % 2 == 0 else b.g[3]
        if hp >= 1:
            b.wait(eng, b.sPE, mm_last.get(hp - 1, 0))
        b.dma(eng, wgs[hp % 2], decg[hp], wsem)
        l2 = b.dma(eng, wus[hp % 2], decu[hp], wsem)
        b.wait("tensor", wsem, l2)
        v = None
        dv = None
        for hi in range(2):
            h = hp * 2 + hi
            for half in range(2):
                base = 4 * half
                b.wait("tensor", b.sDV, dv_half.get((h - 1, half), 0))
                for kc in range(NKC):
                    co = kc * 256 + hi * 128
                    for w in range(2):
                        tw = 2 * half + w
                        b.op("tensor", lambda e, hp_=hp, co_=co, kc_=kc, w_=w, tw_=tw,
                             ba=base: e.matmul(
                                 a.ps[ba + w_][:, :], wgs[hp_ % 2][:, co_:co_ + 128],
                                 a.big[:, kc_ * T + tw_ * 512: kc_ * T + (tw_ + 1) * 512],
                                 start=(kc_ == 0), stop=(kc_ == NKC - 1)))
                    for w in range(2):
                        tw = 2 * half + w
                        v = b.op("tensor", lambda e, hp_=hp, co_=co, kc_=kc, w_=w, tw_=tw,
                                 ba=base: e.matmul(
                                     a.ps[ba + 2 + w_][:, :], wus[hp_ % 2][:, co_:co_ + 128],
                                     a.big[:, kc_ * T + tw_ * 512: kc_ * T + (tw_ + 1) * 512],
                                     start=(kc_ == 0), stop=(kc_ == NKC - 1)),
                                 b.sPE if kc == NKC - 1 and w == 1 else None)
                b.wait("scalar", b.sPE, v)
                prev_dv = dv_half.get((h, half - 1), 0) if half == 1 \
                    else dv_half.get((h - 1, 1), 0)
                b.wait("scalar", b.sDV, prev_dv)
                ev = None
                for w in range(2):
                    ev = b.op("scalar", lambda e, w_=w, ba=base: e.activation(
                        a.gs[w_][:, :], a.ps[ba + w_][:, :], AF.Gelu_apprx_tanh),
                        b.sAC if w == 1 else None)
                b.wait("vector", b.sAC, ev)
                if h >= 2 and half == 0:
                    b.wait("vector", b.g[h % 2], st_done.get(h - 2, 0))
                for w in range(2):
                    tw = 2 * half + w
                    dv = b.op("vector", lambda e, h_=h, w_=w, tw_=tw, ba=base:
                              e.tensor_tensor(
                                  guo[h_ % 2][:, tw_ * 512:(tw_ + 1) * 512],
                                  a.ps[ba + 2 + w_][:, :], a.gs[w_][:, :], ALU.mult),
                              b.sDV if w == 1 else None)
                dv_half[(h, half)] = dv
            b.wait("gpsimd", b.sDV, dv)
            st_done[h] = b.dma("gpsimd", gudram[h, :, :], guo[h % 2][:, :], b.g[h % 2])
        mm_last[hp] = v
    b.barrier()



CONST_KEYS = ["ident_bf", "ident_f", "ones_bf", "onesr_f", "relT", "relT2",
              "masktriT"]


def _const_drams(nc):
    d = {}
    d["ident_bf"] = nc.dram_tensor("ident_bf", [128, 128], BF16, kind="ExternalInput")
    d["ident_f"] = nc.dram_tensor("ident_f", [128, 128], F32, kind="ExternalInput")
    d["ones_bf"] = nc.dram_tensor("ones_bf", [128, 1], BF16, kind="ExternalInput")
    d["onesr_f"] = nc.dram_tensor("onesr_f", [1, 128], F32, kind="ExternalInput")
    d["relT"] = nc.dram_tensor("relT", [128, 512], F32, kind="ExternalInput")
    d["relT2"] = nc.dram_tensor("relT2", [128, 1024], F32, kind="ExternalInput")
    d["masktriT"] = nc.dram_tensor("masktriT", [128, 128], F32, kind="ExternalInput")
    return d


def attn_buffers(b, a):
    bufs = [a.xs[0][:, :].bitcast(BF16), a.xs[1][:, :].bitcast(BF16),
            a.osg[0][:, :].bitcast(BF16)]
    qkvh = [[bufs[t][:, i * T:(i + 1) * T] for t in range(3)] for i in range(2)]
    vtm = a.osg[1][:, :].bitcast(BF16)[:, 0:T]
    rlb = a.osg[1][:, 1024:1536]
    aph = a.qb[1][:, :].bitcast(F32)
    rlrow = a.stats[0:1, 0:512]
    gsc = a.stats[0:1, 600:601]
    return qkvh, vtm, rlb, rlrow, gsc, aph


def build_attn_launch(layer):
    nc = bass.Bass("TRN2", target_bir_lowering=False)
    xin = nc.dram_tensor("xin", [D, T], F32, kind="ExternalInput")
    wqkv = nc.dram_tensor("wqkv", [48, 128, NKC, 128], F32, kind="ExternalInput")
    bqkv = nc.dram_tensor("bqkv", [128, 48], F32, kind="ExternalInput")
    wproj = nc.dram_tensor("wproj", [16, 128, NKC, 128], F32, kind="ExternalInput")
    bproj = nc.dram_tensor("bproj", [128, 16], F32, kind="ExternalInput")
    dt = _const_drams(nc)
    qkvT = nc.dram_tensor("qkvT", [48, 128, T], BF16, kind="Internal")
    xout = nc.dram_tensor("xout", [D, T], F32, kind="ExternalOutput")
    with ExitStack() as st:
        b = B(nc, st)
        a = Arena(b)
        ab = attn_buffers(b, a)
        bq = b.sbuf("bq", [128, 48], F32)
        bp = b.sbuf("bp", [128, 16], F32)
        load_consts(b, a, dt)
        b.dma("sync", bq[:, :], bqkv.ap(), b.q[6])
        v = b.dma("sync", bp[:, :], bproj.ap(), b.q[6])
        for e in ENGINES:
            b.wait(e, b.q[6], v)
        ph_ln(b, a, xin.ap())
        ph_qkv(b, a, wqkv.ap(), bq, qkvT.ap())
        ph_attn(b, a, layer, qkvT.ap(), ab)
        ph_proj(b, a, wproj.ap(), bp, xin.ap(), xout.ap())
        b.finish()
    return nc


def build_ffn0_launch():
    nc = bass.Bass("TRN2", target_bir_lowering=False)
    xin = nc.dram_tensor("xin", [D, T], F32, kind="ExternalInput")
    wg = nc.dram_tensor("wg", [NH0, 128, NKC, 128], I32, kind="ExternalInput")
    wu = nc.dram_tensor("wu", [NH0, 128, NKC, 128], I32, kind="ExternalInput")
    wd = nc.dram_tensor("wd", [4, NH0 // 2, 128, 2, 512], I32, kind="ExternalInput")
    sg = nc.dram_tensor("sg", [128, NH0], F32, kind="ExternalInput")
    su = nc.dram_tensor("su", [128, NH0], F32, kind="ExternalInput")
    sd = nc.dram_tensor("sd", [128, 16], F32, kind="ExternalInput")
    dt = _const_drams(nc)
    gu = nc.dram_tensor("gu", [NH0, 128, T], BF16, kind="Internal")
    xout = nc.dram_tensor("xout", [D, T], F32, kind="ExternalOutput")
    with ExitStack() as st:
        b = B(nc, st)
        a = Arena(b)
        sgb = b.sbuf("sgb", [128, NH0], F32)
        sub = b.sbuf("subb", [128, NH0], F32)
        sdb = b.sbuf("sdbb", [128, 16], F32)
        load_consts(b, a, dt)
        b.dma("sync", sgb[:, :], sg.ap(), b.q[6])
        b.dma("sync", sub[:, :], su.ap(), b.q[6])
        v = b.dma("sync", sdb[:, :], sd.ap(), b.q[6])
        for e in ENGINES:
            b.wait(e, b.q[6], v)
        ph_ln(b, a, xin.ap())
        ph_gateup0(b, a, wg.ap(), wu.ap(), sgb, sub, gu.ap())
        ph_down0(b, a, wd.ap(), sdb, gu.ap(), xin.ap(), xout.ap())
        b.finish()
    return nc


def build_ffn1_launch():
    raise NotImplementedError("ffn1 is fused-only now (collaborative decode)")


def build_fused():
    nc = bass.Bass("TRN2", target_bir_lowering=False)
    xin = nc.dram_tensor("xin", [D, T], F32, kind="ExternalInput")
    w0qkv = nc.dram_tensor("w0qkv", [48, 128, NKC, 128], F32, kind="ExternalInput")
    b0qkv = nc.dram_tensor("b0qkv", [128, 48], F32, kind="ExternalInput")
    w0proj = nc.dram_tensor("w0proj", [16, 128, NKC, 128], F32, kind="ExternalInput")
    b0proj = nc.dram_tensor("b0proj", [128, 16], F32, kind="ExternalInput")
    w1qkv = nc.dram_tensor("w1qkv", [48, 128, NKC, 128], F32, kind="ExternalInput")
    b1qkv = nc.dram_tensor("b1qkv", [128, 48], F32, kind="ExternalInput")
    w1proj = nc.dram_tensor("w1proj", [16, 128, NKC, 128], F32, kind="ExternalInput")
    b1proj = nc.dram_tensor("b1proj", [128, 16], F32, kind="ExternalInput")
    wg = nc.dram_tensor("wg", [NH0, 128, NKC, 128], I32, kind="ExternalInput")
    wu = nc.dram_tensor("wu", [NH0, 128, NKC, 128], I32, kind="ExternalInput")
    wd = nc.dram_tensor("wd", [4, NH0 // 2, 128, 2, 512], I32, kind="ExternalInput")
    sg = nc.dram_tensor("sg", [128, NH0], F32, kind="ExternalInput")
    su = nc.dram_tensor("su", [128, NH0], F32, kind="ExternalInput")
    sd = nc.dram_tensor("sd", [128, 16], F32, kind="ExternalInput")
    gpTs = nc.dram_tensor("gpTs", [1024, 2048], I32, kind="ExternalInput")
    upTs = nc.dram_tensor("upTs", [1024, 2048], I32, kind="ExternalInput")
    gsTs = nc.dram_tensor("gsTs", [64, 2048], F32, kind="ExternalInput")
    usTs = nc.dram_tensor("usTs", [64, 2048], F32, kind="ExternalInput")
    dpTs = nc.dram_tensor("dpTs", [1024, D], I32, kind="ExternalInput")
    dsTs = nc.dram_tensor("dsTs", [64, D], F32, kind="ExternalInput")
    dt = _const_drams(nc)
    qkvT = nc.dram_tensor("qkvT", [48, 128, T], BF16, kind="Internal")
    gu0 = nc.dram_tensor("gu0", [NH0, 128, T], BF16, kind="Internal")
    gu1 = nc.dram_tensor("gu1", [NH1, 128, T], BF16, kind="Internal")
    ing = nc.dram_tensor("ing", [1024, 4096], BF16, kind="Internal")
    inu = nc.dram_tensor("inu", [1024, 4096], BF16, kind="Internal")
    ind = nc.dram_tensor("ind", [2048, 2048], BF16, kind="Internal")
    decg = nc.dram_tensor("decg", [8192, 4096], BF16, kind="Internal")
    decu = nc.dram_tensor("decu", [8192, 4096], BF16, kind="Internal")
    decd = nc.dram_tensor("decd", [16384, 2048], BF16, kind="Internal")
    x1 = nc.dram_tensor("x1", [D, T], F32, kind="Internal")
    x2 = nc.dram_tensor("x2", [D, T], F32, kind="Internal")
    x3 = nc.dram_tensor("x3", [D, T], F32, kind="Internal")
    xout = nc.dram_tensor("xout", [D, T], F32, kind="ExternalOutput")
    with ExitStack() as st:
        b = B(nc, st)
        a = Arena(b)
        ab = attn_buffers(b, a)
        bq0 = b.sbuf("bq0", [128, 48], F32)
        bp0 = b.sbuf("bp0", [128, 16], F32)
        bq1 = b.sbuf("bq1", [128, 48], F32)
        bp1 = b.sbuf("bp1", [128, 16], F32)
        sgb = b.sbuf("sgb", [128, NH0], F32)
        sub = b.sbuf("subb", [128, NH0], F32)
        sdb = b.sbuf("sdbb", [128, 16], F32)
        load_consts(b, a, dt)
        for bf, dr in ((bq0, b0qkv), (bp0, b0proj), (bq1, b1qkv), (bp1, b1proj),
                       (sgb, sg), (sub, su), (sdb, sd)):
            v = b.dma("sync", bf[:, :], dr.ap(), b.q[6])
        for e in ENGINES:
            b.wait(e, b.q[6], v)
        ccsem = Sem(st, nc, "ccsem")
        ph_decode1(b, a, gpTs.ap(), gsTs.ap(), upTs.ap(), usTs.ap(),
                   dpTs.ap(), dsTs.ap(), ing.ap(), inu.ap(), ind.ap())
        decg4 = decg.ap().rearrange("(h p) c -> h p c", h=64)
        decu4 = decu.ap().rearrange("(h p) c -> h p c", h=64)
        decd4 = decd.ap().rearrange("(h p) c -> h p c", h=NH1)
        ph_ln(b, a, xin.ap())
        ph_qkv(b, a, w0qkv.ap(), bq0, qkvT.ap())
        ph_gather(b, ccsem, [(ing.ap().opt(), decg.ap().opt()),
                             (inu.ap().opt(), decu.ap().opt()),
                             (ind.ap().opt(), decd.ap().opt())])
        ph_attn(b, a, 0, qkvT.ap(), ab)
        ph_proj(b, a, w0proj.ap(), bp0, xin.ap(), x1.ap())
        ph_ln(b, a, x1.ap())
        ph_gateup0(b, a, wg.ap(), wu.ap(), sgb, sub, gu0.ap())
        ph_down0(b, a, wd.ap(), sdb, gu0.ap(), x1.ap(), x2.ap())
        ph_ln(b, a, x2.ap())
        ph_qkv(b, a, w1qkv.ap(), bq1, qkvT.ap())
        ph_attn(b, a, 1, qkvT.ap(), ab)
        ph_proj(b, a, w1proj.ap(), bp1, x2.ap(), x3.ap())
        ph_ln(b, a, x3.ap())
        for e in ENGINES:
            b.wait(e, ccsem, 3)
        ph_gateup1(b, a, decg4, decu4, gu1.ap())
        ph_down1(b, a, decd4, gu1.ap(), x3.ap(), xout.ap())
        b.finish()
    return nc



_NC = 8
FUSED = True


def kernel(**inputs):
    p = host_prep(inputs)
    consts = {k: p[k] for k in CONST_KEYS}
    x = np.asarray(inputs["L0_attn_x"], np.float32)
    xs = [np.ascontiguousarray(x[c].T) for c in range(_NC)]
    cores = list(range(_NC))

    def run(nc, shared):
        in_maps = [dict(shared, xin=xs[c]) for c in cores]
        res = run_bass_kernel_spmd(nc, in_maps, core_ids=cores)
        for c in cores:
            xs[c] = np.ascontiguousarray(np.asarray(res.results[c]["xout"]))

    if FUSED:
        shared = dict(
            consts, w0qkv=p["w0qkv"], b0qkv=p["b0qkv"], w0proj=p["w0proj"],
            b0proj=p["b0proj"], w1qkv=p["w1qkv"], b1qkv=p["b1qkv"],
            w1proj=p["w1proj"], b1proj=p["b1proj"], wg=p["wg0"], wu=p["wu0"],
            wd=p["wd0"], sg=p["sg0"], su=p["su0"], sd=p["sd0"])
        nc = build_fused()
        in_maps = []
        for c in cores:
            in_maps.append(dict(
                shared, xin=xs[c],
                gpTs=np.ascontiguousarray(p["gpT"][:, 2048 * c:2048 * (c + 1)]),
                upTs=np.ascontiguousarray(p["upT"][:, 2048 * c:2048 * (c + 1)]),
                gsTs=np.ascontiguousarray(p["gsT"][:, 2048 * c:2048 * (c + 1)]),
                usTs=np.ascontiguousarray(p["usT"][:, 2048 * c:2048 * (c + 1)]),
                dpTs=np.ascontiguousarray(p["dpT"][1024 * c:1024 * (c + 1)]),
                dsTs=np.ascontiguousarray(p["dsT"][64 * c:64 * (c + 1)])))
        res = run_bass_kernel_spmd(nc, in_maps, core_ids=cores)
        for c in cores:
            xs[c] = np.ascontiguousarray(np.asarray(res.results[c]["xout"]))
    else:
        run(build_attn_launch(0), dict(consts, wqkv=p["w0qkv"], bqkv=p["b0qkv"],
                                       wproj=p["w0proj"], bproj=p["b0proj"]))
        run(build_ffn0_launch(), dict(consts, wg=p["wg0"], wu=p["wu0"], wd=p["wd0"],
                                      sg=p["sg0"], su=p["su0"], sd=p["sd0"]))
        run(build_attn_launch(1), dict(consts, wqkv=p["w1qkv"], bqkv=p["b1qkv"],
                                       wproj=p["w1proj"], bproj=p["b1proj"]))
        run(build_ffn1_launch(), dict(consts, gpT=p["gpT"], upT=p["upT"], gsT=p["gsT"],
                                      usT=p["usT"], dpT=p["dpT"], dsT=p["dsT"]))

    return np.stack([xs[c].T for c in cores]).astype(np.float32)


# version 19 (final)
# speedup vs baseline: 1.0275x; 1.0079x over previous
import sys
sys.path.insert(0, '/opt/trn_rl_repo')
from contextlib import ExitStack
import math
import numpy as np
import ml_dtypes
import concourse.bass as bass
import concourse.mybir as mybir
from concourse.bass_utils import run_bass_kernel_spmd

F32 = mybir.dt.float32
BF16 = mybir.dt.bfloat16
I32 = mybir.dt.int32
AF = mybir.ActivationFunctionType
ALU = mybir.AluOpType
AX = mybir.AxisListType
BF = ml_dtypes.bfloat16

D = 2048
T = 2048
NKC = 16
NH = 16
HD = 128
NH0 = 44
H1 = 16384
NH1 = 128
EPS = 1e-5
NEG = -1.0e30

SCALE = {0: 1.0 / math.sqrt(HD), 1: 1.0 / HD}
SLOPES = [2.0 ** (-0.5 * (i + 1)) for i in range(NH)]

ENGINES = ("sync", "gpsimd", "tensor", "scalar", "vector")



def _block_in_out(w, nm):
    return np.ascontiguousarray(w.reshape(NKC, 128, nm, 128).transpose(2, 1, 0, 3))


def _block_out_in(w, nh):
    return np.ascontiguousarray(w.reshape(nh, 128, NKC, 128).transpose(0, 3, 2, 1))


def _block_down0(w):
    return np.ascontiguousarray(
        w.reshape(4, 512, NH0 // 2, 2, 128).transpose(0, 2, 4, 3, 1))


def _perm256(a):
    n = a.shape[0]
    return np.ascontiguousarray(
        a.reshape(n // 256, 128, 2, *a.shape[1:]).swapaxes(1, 2).reshape(a.shape))


def _colT(v):
    return np.ascontiguousarray(v.reshape(-1, 128).T)


def host_prep(inp):
    p = {}
    p["w0qkv"] = _block_in_out(np.asarray(inp["L0_attn_w_attn"], np.float32), 48)
    p["b0qkv"] = _colT(np.asarray(inp["L0_attn_b_attn"], np.float32))
    p["w0proj"] = _block_in_out(np.asarray(inp["L0_attn_w_proj"], np.float32), 16)
    p["b0proj"] = _colT(np.asarray(inp["L0_attn_b_proj"], np.float32))
    p["w1qkv"] = _block_in_out(np.asarray(inp["L1_attn_w_qkv"], np.float32), 48)
    p["b1qkv"] = _colT(np.asarray(inp["L1_attn_b_qkv"], np.float32))
    p["w1proj"] = _block_in_out(np.asarray(inp["L1_attn_w_proj"], np.float32), 16)
    p["b1proj"] = _colT(np.asarray(inp["L1_attn_b_proj"], np.float32))

    p["wg0"] = _block_out_in(np.asarray(inp["L0_ffn_wq_gate"], np.int32), NH0)
    p["wu0"] = _block_out_in(np.asarray(inp["L0_ffn_wq_up"], np.int32), NH0)
    p["wd0"] = _block_down0(np.asarray(inp["L0_ffn_wq_down"], np.int32))
    p["sg0"] = _colT(np.asarray(inp["L0_ffn_s_gate"], np.float32))
    p["su0"] = _colT(np.asarray(inp["L0_ffn_s_up"], np.float32))
    p["sd0"] = _colT(np.asarray(inp["L0_ffn_s_down"], np.float32))

    gp = _perm256(np.asarray(inp["L1_ffn_gate_W_packed"], np.int32))
    up = _perm256(np.asarray(inp["L1_ffn_up_W_packed"], np.int32))
    gs = _perm256(np.asarray(inp["L1_ffn_gate_scales"], np.float32))
    us = _perm256(np.asarray(inp["L1_ffn_up_scales"], np.float32))
    p["gpT"] = np.ascontiguousarray(gp.T)
    p["upT"] = np.ascontiguousarray(up.T)
    p["gsT"] = np.ascontiguousarray(gs.T)
    p["usT"] = np.ascontiguousarray(us.T)
    p["dpT"] = np.ascontiguousarray(np.asarray(inp["L1_ffn_down_W_packed"], np.int32).T)
    p["dsT"] = np.ascontiguousarray(np.asarray(inp["L1_ffn_down_scales"], np.float32).T)

    p["ident_bf"] = np.eye(128, dtype=BF)
    p["ident_f"] = np.eye(128, dtype=np.float32)
    p["ones_bf"] = np.ones((128, 1), dtype=BF)
    p["onesr_f"] = np.ones((1, 128), dtype=np.float32)
    jj, pp = np.meshgrid(np.arange(512), np.arange(128))
    p["relT"] = (pp - jj).astype(np.float32)
    p["relT2"] = np.concatenate([p["relT"], p["relT"] + 128.0], axis=1)
    c, r = np.meshgrid(np.arange(128), np.arange(128))
    p["masktriT"] = np.where(r > c, NEG, 0.0).astype(np.float32)
    return p



class Sem:
    def __init__(self, st, nc, name):
        self.h = st.enter_context(nc.semaphore(name))
        self.v = 0

    def inc(self, n=1):
        self.v += n
        return self.v


class B:
    def __init__(self, nc, st):
        self.nc = nc
        self.st = st
        self.ops = {e: [] for e in ENGINES}
        self.sQ = Sem(st, nc, "sQ")
        self.sG = Sem(st, nc, "sG")
        self.q = [Sem(st, nc, f"q{i}") for i in range(8)]
        self.g = [Sem(st, nc, f"g{i}") for i in range(4)]
        self.sPE = Sem(st, nc, "sPE")
        self.sAC = Sem(st, nc, "sAC")
        self.sDV = Sem(st, nc, "sDV")
        self.pb = Sem(st, nc, "pb")

    def sbuf(self, name, shape, dt):
        return self.st.enter_context(self.nc.sbuf_tensor(name, shape, dt))

    def psum(self, name, shape, dt):
        return self.st.enter_context(self.nc.psum_tensor(name, shape, dt))

    def em(self, eng, fn):
        self.ops[eng].append(fn)

    def wait(self, eng, sem, v):
        if v and v > 0:
            self.em(eng, lambda e, h=sem.h, vv=v: e.wait_ge(h, vv))

    def dma(self, eng, dst, src, sem):
        v = sem.inc(16)
        self.em(eng, lambda e, d=dst, s=src, h=sem.h: e.dma_start(d, s).then_inc(h, 16))
        return v

    def op(self, eng, fn, sem=None, k=1):
        if eng == "vector" or eng == "scalar":
            s = self.sDV if eng == "vector" else self.sAC
            assert sem is None or sem is s
            if s.v > 0:
                self.em(eng, lambda e, h=s.h, vv=s.v: e.wait_ge(h, vv))
            v = s.inc(1)
            self.em(eng, lambda e, f=fn, h=s.h: f(e).then_inc(h, 1))
            return v
        if sem is None:
            self.em(eng, fn)
            return None
        v = sem.inc(k)
        self.em(eng, lambda e, f=fn, h=sem.h, kk=k: f(e).then_inc(h, kk))
        return v

    def barrier(self):
        sems = [self.sQ, self.sG, self.sPE, self.sAC, self.sDV] + self.q + self.g
        vals = [s.v for s in sems]
        target = self.pb.v + len(ENGINES)
        self.pb.v = target
        for eng in ENGINES:
            def _bar(e, sems=sems, vals=vals, pbh=self.pb.h, tgt=target):
                for s, v in zip(sems, vals):
                    if v > 0:
                        e.wait_ge(s.h, v)
                e.nop().then_inc(pbh, 1)
                e.wait_ge(pbh, tgt)
            self.em(eng, _bar)

    def finish(self):
        block = self.st.enter_context(self.nc.Block())
        for eng in ENGINES:
            body = getattr(block, eng)
            lst = self.ops[eng]

            def _run(e, lst=lst):
                for fn in lst:
                    fn(e)
            body(_run)


class Arena:

    def __init__(self, b: B):
        self.big = b.sbuf("big", [128, NKC * T], BF16)
        self.stb = b.sbuf("stb", [128, 8192], F32)
        self.ptb = b.sbuf("ptb", [128, 8192], BF16)
        self.xs = [b.sbuf(f"xs{i}", [128, T], F32) for i in range(2)]
        self.osg = [b.sbuf(f"osg{i}", [128, T], F32) for i in range(2)]
        self.qb = [b.sbuf(f"qbuf{i}", [128, T], BF16) for i in range(2)]
        self.wst2 = [b.sbuf(f"wst2{i}", [128, T], BF16) for i in range(2)]
        self.gs = [b.sbuf(f"gsb{i}", [128, 512], F32) for i in range(2)]
        self.stats = b.sbuf("stats", [1, 4 * T], F32)
        self.tmpf = b.sbuf("tmpf", [128, 512], F32)
        self.small = b.sbuf("small", [128, 40], F32)
        self.ps01 = b.psum("ps01", [128, 1024], F32)
        self.ps = [self.ps01[:, 0:512], self.ps01[:, 512:1024]] + \
            [b.psum(f"ps{i}", [128, 512], F32) for i in range(2, 8)]
        self.ident_bf = b.sbuf("identbf", [128, 128], BF16)
        self.ident_f = b.sbuf("identf", [128, 128], F32)
        self.ones_bf = b.sbuf("onesbf", [128, 1], BF16)
        self.onesr_f = b.sbuf("onesrf", [1, 128], F32)
        self.relT = b.sbuf("relTb", [128, 512], F32)
        self.relT2 = b.sbuf("relT2b", [128, 1024], F32)
        self.mtri = b.sbuf("mtrib", [128, 128], F32)


def load_consts(b: B, a: Arena, dt):
    v = 0
    for nm, buf in (("ident_bf", a.ident_bf), ("ident_f", a.ident_f),
                    ("ones_bf", a.ones_bf), ("onesr_f", a.onesr_f),
                    ("relT", a.relT), ("relT2", a.relT2), ("masktriT", a.mtri)):
        v = b.dma("sync", buf[:, :], dt[nm].ap(), b.q[6])
    return v



def ph_ln(b: B, a: Arena, xdram):
    sq = a.qb[0]
    xld = [b.dma("sync", a.xs[kc][:, :], xdram[kc * 128:(kc + 1) * 128, :], b.q[kc])
           for kc in range(2)]
    pe_kc = [0] * NKC
    for kc in range(NKC):
        b.wait("scalar", b.q[kc % 2], xld[kc])
        if kc >= 1:
            b.wait("scalar", b.sPE, pe_kc[kc - 1])
        bigc = a.big[:, kc * T:(kc + 1) * T]
        b.op("scalar", lambda e, o=bigc, i=a.xs[kc % 2]: e.activation(
            o, i[:, :], AF.Identity))
        v = b.op("scalar", lambda e, i=a.xs[kc % 2]: e.activation(
            sq[:, :], i[:, :], AF.Square), b.sAC)
        if kc + 2 < NKC:
            b.wait("sync", b.sAC, v)
            xld.append(b.dma("sync", a.xs[kc % 2][:, :],
                             xdram[(kc + 2) * 128:(kc + 3) * 128, :], b.q[kc % 2]))
        b.wait("tensor", b.sAC, v)
        for w in range(4):
            b.op("tensor", lambda e, o=bigc, w_=w, s=(kc == 0), t=(kc == NKC - 1):
                 e.matmul(a.ps[w_][0:1, :], a.ones_bf[:, :],
                          o[:, w_ * 512:(w_ + 1) * 512], start=s, stop=t))
            pe_kc[kc] = b.op("tensor", lambda e, w_=w, s=(kc == 0), t=(kc == NKC - 1):
                             e.matmul(a.ps[4 + w_][0:1, :], a.ones_bf[:, :],
                                      sq[:, w_ * 512:(w_ + 1) * 512], start=s, stop=t),
                             b.sPE)
    b.wait("vector", b.sPE, pe_kc[NKC - 1])
    for w in range(4):
        b.op("vector", lambda e, w_=w: e.tensor_copy(
            a.stats[0:1, w_ * 512:(w_ + 1) * 512], a.ps[w_][0:1, :]))
        b.op("vector", lambda e, w_=w: e.tensor_copy(
            a.stats[0:1, T + w_ * 512:T + (w_ + 1) * 512], a.ps[4 + w_][0:1, :]))
    b.op("vector", lambda e: e.tensor_scalar(a.stats[0:1, 0:T], a.stats[0:1, 0:T],
                                             1.0 / D, None, ALU.mult))
    b.op("vector", lambda e: e.tensor_scalar(a.stats[0:1, T:2 * T], a.stats[0:1, T:2 * T],
                                             1.0 / D, None, ALU.mult))
    b.op("vector", lambda e: e.tensor_tensor(a.stats[0:1, 2 * T:3 * T], a.stats[0:1, 0:T],
                                             a.stats[0:1, 0:T], ALU.mult))
    b.op("vector", lambda e: e.tensor_tensor(a.stats[0:1, T:2 * T], a.stats[0:1, T:2 * T],
                                             a.stats[0:1, 2 * T:3 * T], ALU.subtract))
    dv = b.op("vector", lambda e: e.tensor_scalar(a.stats[0:1, T:2 * T],
                                                  a.stats[0:1, T:2 * T], EPS, None,
                                                  ALU.add), b.sDV)
    b.wait("scalar", b.sDV, dv)
    av = b.op("scalar", lambda e: e.activation(a.stats[0:1, 2 * T:3 * T], a.stats[0:1, T:2 * T],
                                               AF.Sqrt), b.sAC)
    b.wait("vector", b.sAC, av)
    b.op("vector", lambda e: e.reciprocal(a.stats[0:1, 2 * T:3 * T], a.stats[0:1, 2 * T:3 * T]))
    dv = b.op("vector", lambda e: e.tensor_tensor(a.stats[0:1, 3 * T:4 * T], a.stats[0:1, 0:T],
                                                  a.stats[0:1, 2 * T:3 * T], ALU.mult), b.sDV)
    b.wait("tensor", b.sDV, dv)
    pv = None
    for w in range(4):
        b.op("tensor", lambda e, w_=w: e.matmul(
            a.ps[w_][:, :], a.onesr_f[:, :], a.stats[0:1, 2 * T + w_ * 512:2 * T + (w_ + 1) * 512],
            start=True, stop=True))
        pv = b.op("tensor", lambda e, w_=w: e.matmul(
            a.ps[4 + w_][:, :], a.onesr_f[:, :], a.stats[0:1, 3 * T + w_ * 512:3 * T + (w_ + 1) * 512],
            start=True, stop=True), b.sPE if w == 3 else None)
    b.wait("vector", b.sPE, pv)
    for w in range(4):
        b.op("vector", lambda e, w_=w: e.tensor_copy(
            a.stb[:, w_ * 512:(w_ + 1) * 512], a.ps[w_][:, :]))
        b.op("vector", lambda e, w_=w: e.tensor_copy(
            a.stb[:, 2048 + w_ * 512:2048 + (w_ + 1) * 512], a.ps[4 + w_][:, :]))
    tmp2 = a.stb[:, 4096:6144]
    for kc in range(NKC):
        sl = slice(kc * T, (kc + 1) * T)
        b.op("vector", lambda e, s=sl: e.tensor_tensor(
            tmp2, a.big[:, s], a.stb[:, 0:2048], ALU.mult))
        b.op("vector", lambda e, s=sl: e.tensor_tensor(
            a.big[:, s], tmp2, a.stb[:, 2048:4096], ALU.subtract))
    b.barrier()


def ph_qkv(b: B, a: Arena, wdram, bcol, qkvT):
    wstrip = a.wst2
    mm_done = [0] * 48
    ev_done = [0] * 48
    st_done = [0] * 48
    for m in range(48):
        if m >= 2:
            b.wait("gpsimd", b.sPE, mm_done[m - 2])
        wld = b.dma("gpsimd",
                    wstrip[m % 2][:, :].rearrange("p (k c) -> p k c", k=NKC),
                    wdram[m], b.g[m % 2])
        b.wait("tensor", b.g[m % 2], wld)
        if m >= 2:
            b.wait("tensor", b.sAC, ev_done[m - 2])
        base = 0 if m % 2 == 0 else 4
        v = None
        for kc in range(NKC):
            for w in range(4):
                v = b.op("tensor", lambda e, w_=w, kc_=kc, m_=m, ba=base: e.matmul(
                    a.ps[ba + w_][:, :],
                    wstrip[m_ % 2][:, kc_ * 128:(kc_ + 1) * 128],
                    a.big[:, kc_ * T + w_ * 512: kc_ * T + (w_ + 1) * 512],
                    start=(kc_ == 0), stop=(kc_ == NKC - 1)),
                    b.sPE if kc == NKC - 1 and w == 3 else None)
        mm_done[m] = v
        b.wait("scalar", b.sPE, v)
        if m >= 2:
            b.wait("scalar", b.q[4 + m % 2], st_done[m - 2])
        for w in range(4):
            v = b.op("scalar", lambda e, w_=w, m_=m, ba=base: e.activation(
                a.qb[m_ % 2][:, w_ * 512:(w_ + 1) * 512], a.ps[ba + w_][:, :],
                AF.Identity, bias=bcol[:, m_:m_ + 1]),
                b.sAC if w == 3 else None)
        ev_done[m] = v
        b.wait("sync", b.sAC, v)
        st_done[m] = b.dma("sync", qkvT[m, :, :], a.qb[m % 2][:, :], b.q[4 + m % 2])
    b.barrier()


def ph_attn(b: B, a: Arena, layer, qkvT, ab):
    qkvh, vtm, rlb, rlrow, gsc, aph = ab
    s = SCALE[layer]
    trps = a.ps[2]
    PS_L, PS_Y, PS_RL = 3, 4, 5
    cur_ld = [b.dma("sync", qkvh[0][t], qkvT[16 * t, :, :], b.q[0]) for t in range(3)]
    head_pe_done = 0
    consumed = {0: (b.sPE, 0), 1: (b.sPE, 0)}
    final_tt = 0
    vtm_copy = 0
    for h in range(NH):
        cur = qkvh[h % 2]
        nxt = qkvh[(h + 1) % 2]
        my_ld = cur_ld
        if h + 1 < NH:
            if h >= 1:
                b.wait("sync", b.sPE, head_pe_done)
            cur_ld = [b.dma("sync", nxt[t], qkvT[16 * t + h + 1, :, :], b.q[(h + 1) % 2])
                      for t in range(3)]
        qh, kh, vh = cur[0], cur[1], cur[2]
        b.wait("tensor", b.q[h % 2], my_ld[2])
        b.wait("vector", b.q[h % 2], my_ld[2])
        for g4 in range(4):
            if h > 0 and g4 == 0:
                b.wait("tensor", b.sDV, final_tt)
            cv = b.op("vector", lambda e, g4_=g4, vh_=vh: e.tensor_copy(
                a.tmpf[:, :], vh_[:, g4_ * 512:(g4_ + 1) * 512]), b.sDV)
            b.wait("tensor", b.sDV, cv)
            if g4 > 0:
                b.wait("tensor", b.sDV, vtm_copy)
            v = None
            for i4 in range(4):
                v = b.op("tensor", lambda e, i4_=i4: e.transpose(
                    trps[:, i4_ * 128:(i4_ + 1) * 128],
                    a.tmpf[:, i4_ * 128:(i4_ + 1) * 128], a.ident_f[:, :]),
                    b.sPE if i4 == 3 else None)
            b.wait("vector", b.sPE, v)
            vtm_copy = b.op("vector", lambda e, g4_=g4: e.tensor_copy(
                vtm[:, g4_ * 512:(g4_ + 1) * 512], trps[:, :]), b.sDV)
        if layer == 1:
            b.op("vector", lambda e, sl=SLOPES[h] / s: e.tensor_scalar(
                aph[:, :], a.relT2[:, :], sl, None, ALU.mult))
        for qb in range(4):
            nk = 4 * qb + 4
            expv = None
            exps = [None] * nk

            def emit_ly(kc, nk=nk):
                b.wait("tensor", b.sAC, exps[kc])
                if kc == 0:
                    b.wait("tensor", b.sDV, final_tt)
                pt = a.ptb[:, kc * 512:(kc + 1) * 512]
                b.op("tensor", lambda e, pt_=pt, s_=(kc == 0), t_=(kc == nk - 1):
                     e.matmul(a.ps[PS_L][0:1, :], a.ones_bf[:, :], pt_,
                              start=s_, stop=t_))
                return b.op("tensor", lambda e, pt_=pt, kc_=kc, s_=(kc == 0),
                            t_=(kc == nk - 1): e.matmul(
                                a.ps[PS_Y][:, :], vtm[:, kc_ * 128:(kc_ + 1) * 128],
                                pt_, start=s_, stop=t_),
                            b.sPE if kc == nk - 1 else None)

            for kc in range(nk):
                par = kc % 2
                psb = a.ps[par]
                csem, cval = consumed[par]
                b.wait("tensor", csem, cval)
                v = b.op("tensor", lambda e, ps=psb, kc_=kc, qb_=qb, kh_=kh, qh_=qh:
                         e.matmul(ps[:, :], kh_[:, kc_ * 128:(kc_ + 1) * 128],
                                  qh_[:, qb_ * 512:(qb_ + 1) * 512],
                                  start=True, stop=True), b.sPE)
                diag = kc >= 4 * qb
                off = 128 * (kc - 4 * qb) if diag else 0
                if layer == 1:
                    b.wait("vector", b.sPE, v)
                    st = a.stb[:, kc * 512:(kc + 1) * 512]
                    Cs = SLOPES[h] * (128 * kc - 512 * qb) / s
                    mk = b.op("vector", lambda e, st_=st, ps=psb, C_=Cs:
                              e.scalar_tensor_tensor(
                                  st_, ps[:, :], C_, aph[:, 0:512],
                                  ALU.add, ALU.add), b.sDV)
                    consumed[par] = (b.sDV, mk)
                    if diag:
                        if off > 0:
                            b.op("vector", lambda e, st_=st, o=off:
                                 e.memset(st_[:, 0:o], NEG))
                        mk = b.op("vector", lambda e, st_=st, o=off:
                                  e.tensor_tensor(
                                      st_[:, o:o + 128], st_[:, o:o + 128],
                                      a.mtri[:, :], ALU.add), b.sDV)
                    b.wait("scalar", b.sDV, mk)
                    expv = b.op("scalar", lambda e, kc_=kc, st_=st: e.activation(
                        a.ptb[:, kc_ * 512:(kc_ + 1) * 512], st_, AF.Exp, scale=s),
                        b.sAC)
                    exps[kc] = expv
                else:
                    if diag:
                        b.wait("vector", b.sPE, v)
                        if off > 0:
                            b.op("vector", lambda e, ps=psb, o=off:
                                 e.memset(ps[:, 0:o], NEG))
                        mk = b.op("vector", lambda e, ps=psb, o=off: e.tensor_tensor(
                            ps[:, o:o + 128], ps[:, o:o + 128], a.mtri[:, :],
                            ALU.add), b.sDV)
                        b.wait("scalar", b.sDV, mk)
                    b.wait("scalar", b.sPE, v)
                    expv = b.op("scalar", lambda e, kc_=kc, ps=psb: e.activation(
                        a.ptb[:, kc_ * 512:(kc_ + 1) * 512], ps[:, :], AF.Exp,
                        scale=s), b.sAC)
                    consumed[par] = (b.sAC, expv)
                    exps[kc] = expv
                if kc >= 2:
                    emit_ly(kc - 2)
            emit_ly(nk - 2)
            v = emit_ly(nk - 1)
            b.wait("vector", b.sPE, v)
            v = b.op("vector", lambda e: e.reciprocal(rlrow[:, :], a.ps[PS_L][0:1, :]),
                     b.sDV)
            b.wait("tensor", b.sDV, v)
            v = b.op("tensor", lambda e: e.matmul(
                a.ps[PS_RL][:, :], a.onesr_f[:, :], rlrow[:, :],
                start=True, stop=True), b.sPE)
            b.wait("vector", b.sPE, v)
            b.op("vector", lambda e: e.tensor_copy(rlb[:, :], a.ps[PS_RL][:, :]))
            final_tt = b.op("vector", lambda e, h_=h, qb_=qb: e.tensor_tensor(
                a.big[:, h_ * T + qb_ * 512: h_ * T + (qb_ + 1) * 512],
                a.ps[PS_Y][:, :], rlb[:, :], ALU.mult), b.sDV)
        head_pe_done = b.sPE.v
    b.barrier()


def ph_proj(b: B, a: Arena, wdram, bcol, xdram, xout):
    wstrip = a.wst2
    xld = [0] * 16
    for m in range(2):
        xld[m] = b.dma("sync", a.xs[m][:, :], xdram[m * 128:(m + 1) * 128, :], b.q[m])
    mm_done = [0] * 16
    ev_done = [0] * 16
    dv_done = [0] * 16
    st_done = [0] * 16
    for m in range(16):
        if m >= 2:
            b.wait("gpsimd", b.sPE, mm_done[m - 2])
        wld = b.dma("gpsimd",
                    wstrip[m % 2][:, :].rearrange("p (k c) -> p k c", k=NKC),
                    wdram[m], b.g[m % 2])
        b.wait("tensor", b.g[m % 2], wld)
        if m >= 2:
            b.wait("tensor", b.sAC, ev_done[m - 2])
        base = 0 if m % 2 == 0 else 4
        v = None
        for kc in range(NKC):
            for w in range(4):
                v = b.op("tensor", lambda e, w_=w, kc_=kc, m_=m, ba=base: e.matmul(
                    a.ps[ba + w_][:, :],
                    wstrip[m_ % 2][:, kc_ * 128:(kc_ + 1) * 128],
                    a.big[:, kc_ * T + w_ * 512: kc_ * T + (w_ + 1) * 512],
                    start=(kc_ == 0), stop=(kc_ == NKC - 1)),
                    b.sPE if kc == NKC - 1 and w == 3 else None)
        mm_done[m] = v
        b.wait("scalar", b.sPE, v)
        if m >= 2:
            b.wait("scalar", b.q[4 + m % 2], st_done[m - 2])
        for w in range(4):
            v = b.op("scalar", lambda e, w_=w, m_=m, ba=base: e.activation(
                a.osg[m_ % 2][:, w_ * 512:(w_ + 1) * 512], a.ps[ba + w_][:, :],
                AF.Identity, bias=bcol[:, m_:m_ + 1]),
                b.sAC if w == 3 else None)
        ev_done[m] = v
        b.wait("vector", b.sAC, v)
        b.wait("vector", b.q[m % 2], xld[m])
        dv_done[m] = b.op("vector", lambda e, m_=m: e.tensor_tensor(
            a.osg[m_ % 2][:, :], a.osg[m_ % 2][:, :], a.xs[m_ % 2][:, :], ALU.add),
            b.sDV)
        if m + 2 < 16:
            b.wait("sync", b.sDV, dv_done[m])
            xld[m + 2] = b.dma("sync", a.xs[m % 2][:, :],
                               xdram[(m + 2) * 128:(m + 3) * 128, :], b.q[m % 2])
        b.wait("sync", b.sDV, dv_done[m])
        st_done[m] = b.dma("sync", xout[m * 128:(m + 1) * 128, :],
                           a.osg[m % 2][:, :], b.q[4 + m % 2])
    b.barrier()


def ph_gateup0(b: B, a: Arena, wg, wu, sgcol, sucol, gudram):
    wst = [a.stb[:, 0:2048].bitcast(I32), a.stb[:, 2048:4096].bitcast(I32)]
    wstu = [a.stb[:, 4096:6144].bitcast(I32), a.stb[:, 6144:8192].bitcast(I32)]
    wbg = [a.ptb[:, 0:2048], a.ptb[:, 2048:4096]]
    wbu = [a.ptb[:, 4096:6144], a.ptb[:, 6144:8192]]
    guo = a.qb
    mm_half = {}
    dv_half = {}
    st_done = [0] * NH0
    deq_done = [0] * NH0
    for h in range(NH0):
        if h >= 2:
            b.wait("sync", b.sDV, deq_done[h - 2])
        b.dma("sync", wst[h % 2].rearrange("p (k c) -> p k c", k=NKC),
              wg[h], b.q[h % 2])
        l2 = b.dma("sync", wstu[h % 2].rearrange("p (k c) -> p k c", k=NKC),
                   wu[h], b.q[h % 2])
        b.wait("vector", b.q[h % 2], l2)
        if h >= 2:
            b.wait("vector", b.sPE, mm_half.get((h - 2, 1), 0))
        b.op("vector", lambda e, h_=h: e.tensor_scalar(
            wbg[h_ % 2], wst[h_ % 2], -128, None, ALU.add))
        deq_done[h] = b.op("vector", lambda e, h_=h: e.tensor_scalar(
            wbu[h_ % 2], wstu[h_ % 2], -128, None, ALU.add), b.sDV)
        b.wait("tensor", b.sDV, deq_done[h])
        for half in range(2):
            base = 4 * half
            if h >= 1:
                b.wait("tensor", b.sDV, dv_half.get((h - 1, half), 0))
            v = None
            for kc in range(NKC):
                for w in range(2):
                    tw = 2 * half + w
                    b.op("tensor", lambda e, h_=h, kc_=kc, w_=w, tw_=tw, ba=base:
                         e.matmul(a.ps[ba + w_][:, :],
                                  wbg[h_ % 2][:, kc_ * 128:(kc_ + 1) * 128],
                                  a.big[:, kc_ * T + tw_ * 512: kc_ * T + (tw_ + 1) * 512],
                                  start=(kc_ == 0), stop=(kc_ == NKC - 1)))
                for w in range(2):
                    tw = 2 * half + w
                    v = b.op("tensor", lambda e, h_=h, kc_=kc, w_=w, tw_=tw, ba=base:
                             e.matmul(a.ps[ba + 2 + w_][:, :],
                                      wbu[h_ % 2][:, kc_ * 128:(kc_ + 1) * 128],
                                      a.big[:, kc_ * T + tw_ * 512: kc_ * T + (tw_ + 1) * 512],
                                      start=(kc_ == 0), stop=(kc_ == NKC - 1)),
                             b.sPE if kc == NKC - 1 and w == 1 else None)
            mm_half[(h, half)] = v
            b.wait("scalar", b.sPE, v)
            prev_dv = dv_half.get((h, half - 1), 0) if half == 1 \
                else dv_half.get((h - 1, 1), 0)
            b.wait("scalar", b.sDV, prev_dv)
            v = None
            for w in range(2):
                v = b.op("scalar", lambda e, h_=h, w_=w, ba=base: e.activation(
                    a.gs[w_][:, :], a.ps[ba + w_][:, :], AF.Silu,
                    scale=sgcol[:, h_:h_ + 1]), b.sAC if w == 1 else None)
            b.wait("vector", b.sAC, v)
            if h >= 2 and half == 0:
                b.wait("vector", b.g[h % 2], st_done[h - 2])
            for w in range(2):
                tw = 2 * half + w
                v = b.op("vector", lambda e, h_=h, w_=w, tw_=tw, ba=base:
                         e.scalar_tensor_tensor(
                             guo[h_ % 2][:, tw_ * 512:(tw_ + 1) * 512],
                             a.ps[ba + 2 + w_][:, :], sucol[:, h_:h_ + 1],
                             a.gs[w_][:, :], ALU.mult, ALU.mult),
                         b.sDV if w == 1 else None)
            dv_half[(h, half)] = v
        b.wait("gpsimd", b.sDV, v)
        st_done[h] = b.dma("gpsimd", gudram[h, :, :], guo[h % 2][:, :], b.g[h % 2])
    b.barrier()


def _down_common(b, a, nh, wpair_fn, gudram, xdram, xout, sdcol):
    gub = [a.ptb[:, 0:2048], a.ptb[:, 2048:4096]]
    st_prev = {}
    pg = {0: 0, 1: 0}
    nhp = nh // 2
    gpair = 0
    for w2 in range(2):
        for dg in range(4):
            last_pe = 0
            grp_start_ac = b.sAC.v
            for hp in range(nhp):
                par = gpair % 2
                gpair += 1
                geng = "sync" if par == 0 else "gpsimd"
                gsem = b.q[2] if par == 0 else b.g[2]
                b.wait(geng, b.sPE, pg[par])
                wsem, wv, wpair = wpair_fn(hp, dg, par, pg[par])
                guld = b.dma(geng,
                             gub[par].rearrange("p (h c) -> p h c", h=2),
                             gudram[2 * hp:2 * hp + 2, :,
                                    w2 * 1024:(w2 + 1) * 1024]
                             .rearrange("h p c -> p h c"), gsem)
                b.wait("tensor", wsem, wv)
                b.wait("tensor", gsem, guld)
                if hp == 0:
                    b.wait("tensor", b.sAC, grp_start_ac)
                v = None
                for hi in range(2):
                    h = 2 * hp + hi
                    for d in range(4):
                        for c in range(2):
                            v = b.op("tensor", lambda e, wp=wpair, hi_=hi, h_=h,
                                     d_=d, c_=c, par_=par: e.matmul(
                                         a.ps[2 * d_ + c_][:, :],
                                         wp[:, hi_ * 512 + d_ * 128:
                                            hi_ * 512 + (d_ + 1) * 128],
                                         gub[par_][:, hi_ * 1024 + c_ * 512:
                                                   hi_ * 1024 + (c_ + 1) * 512],
                                         start=(h_ == 0), stop=(h_ == nh - 1)),
                                     b.sPE if hi == 1 and d == 3 and c == 1 else None)
                pg[par] = v
                last_pe = v
            b.wait("scalar", b.sPE, last_pe)
            for d in range(4):
                dout = 4 * dg + d
                b.wait("sync", b.sDV, st_prev.get(("x", d % 2), 0))
                xv = b.dma("sync", a.xs[d % 2][:, 0:1024],
                           xdram[dout * 128:(dout + 1) * 128, w2 * 1024:(w2 + 1) * 1024],
                           b.q[4 + d % 2])
                b.wait("scalar", b.g[d % 2], st_prev.get(("o", d % 2), 0))
                v = None
                for c in range(2):
                    kw = {}
                    if sdcol is not None:
                        kw["scale"] = sdcol[:, dout:dout + 1]
                    v = b.op("scalar", lambda e, d_=d, c_=c, kw_=kw: e.activation(
                        a.osg[d_ % 2][:, c_ * 512:(c_ + 1) * 512],
                        a.ps[2 * d_ + c_][:, :], AF.Identity, **kw_),
                        b.sAC if c == 1 else None)
                b.wait("vector", b.sAC, v)
                b.wait("vector", b.q[4 + d % 2], xv)
                dv = b.op("vector", lambda e, d_=d: e.tensor_tensor(
                    a.osg[d_ % 2][:, 0:1024], a.osg[d_ % 2][:, 0:1024],
                    a.xs[d_ % 2][:, 0:1024], ALU.add), b.sDV)
                st_prev[("x", d % 2)] = dv
                b.wait("gpsimd", b.sDV, dv)
                st_prev[("o", d % 2)] = b.dma(
                    "gpsimd", xout[dout * 128:(dout + 1) * 128, w2 * 1024:(w2 + 1) * 1024],
                    a.osg[d % 2][:, 0:1024], b.g[d % 2])
    b.barrier()


def ph_down0(b: B, a: Arena, wd, sdcol, gudram, xdram, xout):
    wst = [a.stb[:, 0:1024].bitcast(I32), a.stb[:, 1024:2048].bitcast(I32)]
    wbb = [a.ptb[:, 4096:5120], a.ptb[:, 5120:6144]]
    deq_h = {0: 0, 1: 0}

    def wpair(hp, dg, par, pe_guard):
        b.wait("scalar", b.sDV, deq_h[par])
        b.dma("scalar", wst[par].rearrange("p (h c) -> p h c", h=2),
              wd[dg, hp, :, :, :], b.q[6 + par])
        lv = b.q[6 + par].v
        b.wait("vector", b.q[6 + par], lv)
        b.wait("vector", b.sPE, pe_guard)
        deq_h[par] = b.op("vector", lambda e, p_=par: e.tensor_scalar(
            wbb[p_], wst[p_], -128, None, ALU.add), b.sDV)
        return b.sDV, deq_h[par], wbb[par]

    _down_common(b, a, NH0, wpair, gudram, xdram, xout, sdcol)


def ph_down1(b: B, a: Arena, decd, gudram, xdram, xout):
    wbb = [a.ptb[:, 4096:5120], a.ptb[:, 5120:6144]]

    def wpair(hp, dg, par, pe_guard):
        b.wait("scalar", b.sPE, pe_guard)
        v = b.dma("scalar", wbb[par].rearrange("p (h c) -> p h c", h=2),
                  decd[2 * hp:2 * hp + 2, :, 512 * dg:512 * (dg + 1)]
                  .rearrange("h p c -> p h c"), b.q[6 + par])
        return b.q[6 + par], v, wbb[par]

    _down_common(b, a, NH1, wpair, gudram, xdram, xout, None)


def ph_decode1(b: B, a: Arena, gpTs, gsTs, upTs, usTs, dpTs, dsTs,
               ing, inu, ind):
    pk = [a.xs[0][:, :].bitcast(I32), a.xs[1][:, :].bitcast(I32)]
    sct = [a.osg[0][:, :], a.osg[1][:, :]]
    t0 = a.stb[:, 0:2048].bitcast(I32)
    t3 = a.stb[:, 2048:4096].bitcast(I32)
    t1 = a.stb[:, 4096:6144].bitcast(I32)
    t2 = a.stb[:, 6144:8192].bitcast(I32)
    bigf = a.big[:, :].bitcast(F32)
    cf = bigf[:, 0:2048]
    sg = bigf[:, 2048:4096]
    v0 = bigf[:, 4096:6144]
    wo = [a.ptb[:, i * 2048:(i + 1) * 2048] for i in range(4)]
    last_par = {0: 0, 1: 0}
    prev_vmul = [0]

    def decode_unit(packed_ap, scale_ap, out_lo, out_hi, parity):
        b.wait("sync", b.sDV, last_par[parity])
        b.dma("sync", pk[parity], packed_ap, b.q[parity])
        ls = b.dma("sync", sct[parity], scale_ap, b.q[parity])
        b.wait("vector", b.q[parity], ls)
        b.wait("vector", b.g[parity], b.g[parity].v)
        for hi in range(2):
            wbuf = wo[2 * hi + parity]
            if hi == 0:
                b.op("vector", lambda e, p_=parity: e.tensor_scalar(
                    t0, pk[p_], 7, 22, ALU.bitwise_and, ALU.logical_shift_left))
                mka = b.op("vector", lambda e, p_=parity: e.tensor_scalar(
                    t3, pk[p_], 8, None, ALU.bitwise_and), b.sDV)
                sgm = -0.25
            else:
                b.op("vector", lambda e, p_=parity: e.tensor_scalar(
                    t0, pk[p_], 0x70, 18, ALU.bitwise_and, ALU.logical_shift_left))
                mka = b.op("vector", lambda e, p_=parity: e.tensor_scalar(
                    t3, pk[p_], 128, None, ALU.bitwise_and), b.sDV)
                sgm = -0.25 / 16.0
            b.wait("scalar", b.sDV, mka)
            b.wait("scalar", b.sDV, prev_vmul[0])
            b.op("scalar", lambda e, sgm_=sgm: e.activation(
                sg, t3, AF.Copy, scale=sgm_, bias=1.0))
            b.op("vector", lambda e: e.tensor_scalar(
                t1, t0, 0x3F000000, None, ALU.add))
            b.op("vector", lambda e: e.tensor_scalar(
                t2, t0, 2 << 22, None, ALU.min))
            mkb = b.op("vector", lambda e: e.tensor_scalar(
                t2, t2, -1, 2 << 22, ALU.mult, ALU.add), b.sDV)
            b.wait("scalar", b.sDV, mkb)
            cv = b.op("scalar", lambda e: e.activation(
                cf, t2, AF.Copy, scale=float(0.25 / (1 << 22))), b.sAC)
            b.wait("vector", b.sAC, cv)
            b.op("vector", lambda e: e.tensor_tensor(
                v0, t1.bitcast(F32), cf, ALU.subtract))
            b.op("vector", lambda e: e.tensor_tensor(v0, v0, sg, ALU.mult))
            stv = b.op("vector", lambda e, wb=wbuf, p_=parity: e.tensor_tensor(
                wb, v0, sct[p_], ALU.mult), b.sDV)
            prev_vmul[0] = stv
            b.wait("gpsimd", b.sDV, stv)
            for dst, src_fn in (out_lo if hi == 0 else out_hi):
                b.dma("gpsimd", dst, src_fn(wbuf), b.g[parity])
        last_par[parity] = prev_vmul[0]

    for pT, sT, slab in ((gpTs, gsTs, ing), (upTs, usTs, inu)):
        slab5 = slab.rearrange("(h r two) (k c) -> two r h k c",
                               h=8, two=2, k=NKC)
        for jj in range(8):
            packed = pT[jj * 128:(jj + 1) * 128, :]
            scl = sT[jj * 8:(jj + 1) * 8, :].unsqueeze(1).broadcast_to([8, 16, 2048])
            out_lo = [
                (slab5[0, :, :, 2 * jj, :],
                 lambda w: w[0:64, :].rearrange("p (h c) -> p h c", h=8)),
                (slab5[0, :, :, 2 * jj + 1, :],
                 lambda w: w[64:128, :].rearrange("p (h c) -> p h c", h=8)),
            ]
            out_hi = [
                (slab5[1, :, :, 2 * jj, :],
                 lambda w: w[0:64, :].rearrange("p (h c) -> p h c", h=8)),
                (slab5[1, :, :, 2 * jj + 1, :],
                 lambda w: w[64:128, :].rearrange("p (h c) -> p h c", h=8)),
            ]
            decode_unit(packed, scl, out_lo, out_hi, jj % 2)
    for jj in range(8):
        packed = dpTs[jj * 128:(jj + 1) * 128, :]
        scl = dsTs[jj * 8:(jj + 1) * 8, :].unsqueeze(1).broadcast_to([8, 16, 2048])
        out_lo = [(ind[256 * jj:256 * jj + 128, :], lambda w: w)]
        out_hi = [(ind[256 * jj + 128:256 * jj + 256, :], lambda w: w)]
        decode_unit(packed, scl, out_lo, out_hi, jj % 2)
    b.wait("gpsimd", b.g[0], b.g[0].v)
    b.wait("gpsimd", b.g[1], b.g[1].v)
    b.barrier()


def ph_gather(b: B, ccsem, pairs):
    for src_ap, dst_ap in pairs:
        b.op("gpsimd", lambda e, s=src_ap, d=dst_ap: e.collective_compute(
            "AllGather", ALU.bypass, replica_groups=[list(range(8))],
            ins=[s], outs=[d]), ccsem)
    b.barrier()


def ph_gateup1(b: B, a: Arena, decg, decu, gudram):
    wgs = [a.stb[:, 0:2048].bitcast(BF16), a.stb[:, 2048:4096].bitcast(BF16)]
    wus = [a.stb[:, 4096:6144].bitcast(BF16), a.stb[:, 6144:8192].bitcast(BF16)]
    guo = a.qb
    mm_last = {}
    dv_half = {}
    st_done = {}
    for hp in range(64):
        eng = "sync" if hp % 2 == 0 else "scalar"
        wsem = b.q[0] if hp % 2 == 0 else b.q[7]
        if hp >= 1:
            b.wait(eng, b.sPE, mm_last.get(hp - 1, 0))
        b.dma(eng, wgs[hp % 2], decg[hp], wsem)
        l2 = b.dma(eng, wus[hp % 2], decu[hp], wsem)
        b.wait("tensor", wsem, l2)
        v = None
        dv = None
        for hi in range(2):
            h = hp * 2 + hi
            for half in range(2):
                base = 4 * half
                b.wait("tensor", b.sDV, dv_half.get((h - 1, half), 0))
                for kc in range(NKC):
                    co = kc * 256 + hi * 128
                    for w in range(2):
                        tw = 2 * half + w
                        b.op("tensor", lambda e, hp_=hp, co_=co, kc_=kc, w_=w, tw_=tw,
                             ba=base: e.matmul(
                                 a.ps[ba + w_][:, :], wgs[hp_ % 2][:, co_:co_ + 128],
                                 a.big[:, kc_ * T + tw_ * 512: kc_ * T + (tw_ + 1) * 512],
                                 start=(kc_ == 0), stop=(kc_ == NKC - 1)))
                    for w in range(2):
                        tw = 2 * half + w
                        v = b.op("tensor", lambda e, hp_=hp, co_=co, kc_=kc, w_=w, tw_=tw,
                                 ba=base: e.matmul(
                                     a.ps[ba + 2 + w_][:, :], wus[hp_ % 2][:, co_:co_ + 128],
                                     a.big[:, kc_ * T + tw_ * 512: kc_ * T + (tw_ + 1) * 512],
                                     start=(kc_ == 0), stop=(kc_ == NKC - 1)),
                                 b.sPE if kc == NKC - 1 and w == 1 else None)
                b.wait("scalar", b.sPE, v)
                prev_dv = dv_half.get((h, half - 1), 0) if half == 1 \
                    else dv_half.get((h - 1, 1), 0)
                b.wait("scalar", b.sDV, prev_dv)
                ev = None
                for w in range(2):
                    ev = b.op("scalar", lambda e, w_=w, ba=base: e.activation(
                        a.gs[w_][:, :], a.ps[ba + w_][:, :], AF.Gelu_apprx_tanh),
                        b.sAC if w == 1 else None)
                b.wait("vector", b.sAC, ev)
                if h >= 2 and half == 0:
                    b.wait("vector", b.g[h % 2], st_done.get(h - 2, 0))
                for w in range(2):
                    tw = 2 * half + w
                    dv = b.op("vector", lambda e, h_=h, w_=w, tw_=tw, ba=base:
                              e.tensor_tensor(
                                  guo[h_ % 2][:, tw_ * 512:(tw_ + 1) * 512],
                                  a.ps[ba + 2 + w_][:, :], a.gs[w_][:, :], ALU.mult),
                              b.sDV if w == 1 else None)
                dv_half[(h, half)] = dv
            b.wait("gpsimd", b.sDV, dv)
            st_done[h] = b.dma("gpsimd", gudram[h, :, :], guo[h % 2][:, :], b.g[h % 2])
        mm_last[hp] = v
    b.barrier()



CONST_KEYS = ["ident_bf", "ident_f", "ones_bf", "onesr_f", "relT", "relT2",
              "masktriT"]


def _const_drams(nc):
    d = {}
    d["ident_bf"] = nc.dram_tensor("ident_bf", [128, 128], BF16, kind="ExternalInput")
    d["ident_f"] = nc.dram_tensor("ident_f", [128, 128], F32, kind="ExternalInput")
    d["ones_bf"] = nc.dram_tensor("ones_bf", [128, 1], BF16, kind="ExternalInput")
    d["onesr_f"] = nc.dram_tensor("onesr_f", [1, 128], F32, kind="ExternalInput")
    d["relT"] = nc.dram_tensor("relT", [128, 512], F32, kind="ExternalInput")
    d["relT2"] = nc.dram_tensor("relT2", [128, 1024], F32, kind="ExternalInput")
    d["masktriT"] = nc.dram_tensor("masktriT", [128, 128], F32, kind="ExternalInput")
    return d


def attn_buffers(b, a):
    bufs = [a.xs[0][:, :].bitcast(BF16), a.xs[1][:, :].bitcast(BF16),
            a.osg[0][:, :].bitcast(BF16)]
    qkvh = [[bufs[t][:, i * T:(i + 1) * T] for t in range(3)] for i in range(2)]
    vtm = a.osg[1][:, :].bitcast(BF16)[:, 0:T]
    rlb = a.osg[1][:, 1024:1536]
    aph = a.qb[1][:, :].bitcast(F32)
    rlrow = a.stats[0:1, 0:512]
    gsc = a.stats[0:1, 600:601]
    return qkvh, vtm, rlb, rlrow, gsc, aph


def build_attn_launch(layer):
    nc = bass.Bass("TRN2", target_bir_lowering=False)
    xin = nc.dram_tensor("xin", [D, T], F32, kind="ExternalInput")
    wqkv = nc.dram_tensor("wqkv", [48, 128, NKC, 128], F32, kind="ExternalInput")
    bqkv = nc.dram_tensor("bqkv", [128, 48], F32, kind="ExternalInput")
    wproj = nc.dram_tensor("wproj", [16, 128, NKC, 128], F32, kind="ExternalInput")
    bproj = nc.dram_tensor("bproj", [128, 16], F32, kind="ExternalInput")
    dt = _const_drams(nc)
    qkvT = nc.dram_tensor("qkvT", [48, 128, T], BF16, kind="Internal")
    xout = nc.dram_tensor("xout", [D, T], F32, kind="ExternalOutput")
    with ExitStack() as st:
        b = B(nc, st)
        a = Arena(b)
        ab = attn_buffers(b, a)
        bq = b.sbuf("bq", [128, 48], F32)
        bp = b.sbuf("bp", [128, 16], F32)
        load_consts(b, a, dt)
        b.dma("sync", bq[:, :], bqkv.ap(), b.q[6])
        v = b.dma("sync", bp[:, :], bproj.ap(), b.q[6])
        for e in ENGINES:
            b.wait(e, b.q[6], v)
        ph_ln(b, a, xin.ap())
        ph_qkv(b, a, wqkv.ap(), bq, qkvT.ap())
        ph_attn(b, a, layer, qkvT.ap(), ab)
        ph_proj(b, a, wproj.ap(), bp, xin.ap(), xout.ap())
        b.finish()
    return nc


def build_ffn0_launch():
    nc = bass.Bass("TRN2", target_bir_lowering=False)
    xin = nc.dram_tensor("xin", [D, T], F32, kind="ExternalInput")
    wg = nc.dram_tensor("wg", [NH0, 128, NKC, 128], I32, kind="ExternalInput")
    wu = nc.dram_tensor("wu", [NH0, 128, NKC, 128], I32, kind="ExternalInput")
    wd = nc.dram_tensor("wd", [4, NH0 // 2, 128, 2, 512], I32, kind="ExternalInput")
    sg = nc.dram_tensor("sg", [128, NH0], F32, kind="ExternalInput")
    su = nc.dram_tensor("su", [128, NH0], F32, kind="ExternalInput")
    sd = nc.dram_tensor("sd", [128, 16], F32, kind="ExternalInput")
    dt = _const_drams(nc)
    gu = nc.dram_tensor("gu", [NH0, 128, T], BF16, kind="Internal")
    xout = nc.dram_tensor("xout", [D, T], F32, kind="ExternalOutput")
    with ExitStack() as st:
        b = B(nc, st)
        a = Arena(b)
        sgb = b.sbuf("sgb", [128, NH0], F32)
        sub = b.sbuf("subb", [128, NH0], F32)
        sdb = b.sbuf("sdbb", [128, 16], F32)
        load_consts(b, a, dt)
        b.dma("sync", sgb[:, :], sg.ap(), b.q[6])
        b.dma("sync", sub[:, :], su.ap(), b.q[6])
        v = b.dma("sync", sdb[:, :], sd.ap(), b.q[6])
        for e in ENGINES:
            b.wait(e, b.q[6], v)
        ph_ln(b, a, xin.ap())
        ph_gateup0(b, a, wg.ap(), wu.ap(), sgb, sub, gu.ap())
        ph_down0(b, a, wd.ap(), sdb, gu.ap(), xin.ap(), xout.ap())
        b.finish()
    return nc


def build_ffn1_launch():
    raise NotImplementedError("ffn1 is fused-only now (collaborative decode)")


def build_fused():
    nc = bass.Bass("TRN2", target_bir_lowering=False)
    xin = nc.dram_tensor("xin", [D, T], F32, kind="ExternalInput")
    w0qkv = nc.dram_tensor("w0qkv", [48, 128, NKC, 128], F32, kind="ExternalInput")
    b0qkv = nc.dram_tensor("b0qkv", [128, 48], F32, kind="ExternalInput")
    w0proj = nc.dram_tensor("w0proj", [16, 128, NKC, 128], F32, kind="ExternalInput")
    b0proj = nc.dram_tensor("b0proj", [128, 16], F32, kind="ExternalInput")
    w1qkv = nc.dram_tensor("w1qkv", [48, 128, NKC, 128], F32, kind="ExternalInput")
    b1qkv = nc.dram_tensor("b1qkv", [128, 48], F32, kind="ExternalInput")
    w1proj = nc.dram_tensor("w1proj", [16, 128, NKC, 128], F32, kind="ExternalInput")
    b1proj = nc.dram_tensor("b1proj", [128, 16], F32, kind="ExternalInput")
    wg = nc.dram_tensor("wg", [NH0, 128, NKC, 128], I32, kind="ExternalInput")
    wu = nc.dram_tensor("wu", [NH0, 128, NKC, 128], I32, kind="ExternalInput")
    wd = nc.dram_tensor("wd", [4, NH0 // 2, 128, 2, 512], I32, kind="ExternalInput")
    sg = nc.dram_tensor("sg", [128, NH0], F32, kind="ExternalInput")
    su = nc.dram_tensor("su", [128, NH0], F32, kind="ExternalInput")
    sd = nc.dram_tensor("sd", [128, 16], F32, kind="ExternalInput")
    gpTs = nc.dram_tensor("gpTs", [1024, 2048], I32, kind="ExternalInput")
    upTs = nc.dram_tensor("upTs", [1024, 2048], I32, kind="ExternalInput")
    gsTs = nc.dram_tensor("gsTs", [64, 2048], F32, kind="ExternalInput")
    usTs = nc.dram_tensor("usTs", [64, 2048], F32, kind="ExternalInput")
    dpTs = nc.dram_tensor("dpTs", [1024, D], I32, kind="ExternalInput")
    dsTs = nc.dram_tensor("dsTs", [64, D], F32, kind="ExternalInput")
    dt = _const_drams(nc)
    qkvT = nc.dram_tensor("qkvT", [48, 128, T], BF16, kind="Internal")
    gu0 = nc.dram_tensor("gu0", [NH0, 128, T], BF16, kind="Internal")
    gu1 = nc.dram_tensor("gu1", [NH1, 128, T], BF16, kind="Internal")
    ing = nc.dram_tensor("ing", [1024, 4096], BF16, kind="Internal")
    inu = nc.dram_tensor("inu", [1024, 4096], BF16, kind="Internal")
    ind = nc.dram_tensor("ind", [2048, 2048], BF16, kind="Internal")
    decg = nc.dram_tensor("decg", [8192, 4096], BF16, kind="Internal")
    decu = nc.dram_tensor("decu", [8192, 4096], BF16, kind="Internal")
    decd = nc.dram_tensor("decd", [16384, 2048], BF16, kind="Internal")
    x1 = nc.dram_tensor("x1", [D, T], F32, kind="Internal")
    x2 = nc.dram_tensor("x2", [D, T], F32, kind="Internal")
    x3 = nc.dram_tensor("x3", [D, T], F32, kind="Internal")
    xout = nc.dram_tensor("xout", [D, T], F32, kind="ExternalOutput")
    with ExitStack() as st:
        b = B(nc, st)
        a = Arena(b)
        ab = attn_buffers(b, a)
        bq0 = b.sbuf("bq0", [128, 48], F32)
        bp0 = b.sbuf("bp0", [128, 16], F32)
        bq1 = b.sbuf("bq1", [128, 48], F32)
        bp1 = b.sbuf("bp1", [128, 16], F32)
        sgb = b.sbuf("sgb", [128, NH0], F32)
        sub = b.sbuf("subb", [128, NH0], F32)
        sdb = b.sbuf("sdbb", [128, 16], F32)
        load_consts(b, a, dt)
        for bf, dr in ((bq0, b0qkv), (bp0, b0proj), (bq1, b1qkv), (bp1, b1proj),
                       (sgb, sg), (sub, su), (sdb, sd)):
            v = b.dma("sync", bf[:, :], dr.ap(), b.q[6])
        for e in ENGINES:
            b.wait(e, b.q[6], v)
        ccsem = Sem(st, nc, "ccsem")
        ph_decode1(b, a, gpTs.ap(), gsTs.ap(), upTs.ap(), usTs.ap(),
                   dpTs.ap(), dsTs.ap(), ing.ap(), inu.ap(), ind.ap())
        decg4 = decg.ap().rearrange("(h p) c -> h p c", h=64)
        decu4 = decu.ap().rearrange("(h p) c -> h p c", h=64)
        decd4 = decd.ap().rearrange("(h p) c -> h p c", h=NH1)
        ph_ln(b, a, xin.ap())
        ph_qkv(b, a, w0qkv.ap(), bq0, qkvT.ap())
        ph_gather(b, ccsem, [(ing.ap().opt(), decg.ap().opt()),
                             (inu.ap().opt(), decu.ap().opt()),
                             (ind.ap().opt(), decd.ap().opt())])
        ph_attn(b, a, 0, qkvT.ap(), ab)
        ph_proj(b, a, w0proj.ap(), bp0, xin.ap(), x1.ap())
        ph_ln(b, a, x1.ap())
        ph_gateup0(b, a, wg.ap(), wu.ap(), sgb, sub, gu0.ap())
        ph_down0(b, a, wd.ap(), sdb, gu0.ap(), x1.ap(), x2.ap())
        ph_ln(b, a, x2.ap())
        ph_qkv(b, a, w1qkv.ap(), bq1, qkvT.ap())
        ph_attn(b, a, 1, qkvT.ap(), ab)
        ph_proj(b, a, w1proj.ap(), bp1, x2.ap(), x3.ap())
        ph_ln(b, a, x3.ap())
        for e in ENGINES:
            b.wait(e, ccsem, 3)
        ph_gateup1(b, a, decg4, decu4, gu1.ap())
        ph_down1(b, a, decd4, gu1.ap(), x3.ap(), xout.ap())
        b.finish()
    return nc



_NC = 8
FUSED = True


def kernel(**inputs):
    p = host_prep(inputs)
    consts = {k: p[k] for k in CONST_KEYS}
    x = np.asarray(inputs["L0_attn_x"], np.float32)
    xs = [np.ascontiguousarray(x[c].T) for c in range(_NC)]
    cores = list(range(_NC))

    def run(nc, shared):
        in_maps = [dict(shared, xin=xs[c]) for c in cores]
        res = run_bass_kernel_spmd(nc, in_maps, core_ids=cores)
        for c in cores:
            xs[c] = np.ascontiguousarray(np.asarray(res.results[c]["xout"]))

    if FUSED:
        shared = dict(
            consts, w0qkv=p["w0qkv"], b0qkv=p["b0qkv"], w0proj=p["w0proj"],
            b0proj=p["b0proj"], w1qkv=p["w1qkv"], b1qkv=p["b1qkv"],
            w1proj=p["w1proj"], b1proj=p["b1proj"], wg=p["wg0"], wu=p["wu0"],
            wd=p["wd0"], sg=p["sg0"], su=p["su0"], sd=p["sd0"])
        nc = build_fused()
        in_maps = []
        for c in cores:
            in_maps.append(dict(
                shared, xin=xs[c],
                gpTs=np.ascontiguousarray(p["gpT"][:, 2048 * c:2048 * (c + 1)]),
                upTs=np.ascontiguousarray(p["upT"][:, 2048 * c:2048 * (c + 1)]),
                gsTs=np.ascontiguousarray(p["gsT"][:, 2048 * c:2048 * (c + 1)]),
                usTs=np.ascontiguousarray(p["usT"][:, 2048 * c:2048 * (c + 1)]),
                dpTs=np.ascontiguousarray(p["dpT"][1024 * c:1024 * (c + 1)]),
                dsTs=np.ascontiguousarray(p["dsT"][64 * c:64 * (c + 1)])))
        res = run_bass_kernel_spmd(nc, in_maps, core_ids=cores)
        for c in cores:
            xs[c] = np.ascontiguousarray(np.asarray(res.results[c]["xout"]))
    else:
        run(build_attn_launch(0), dict(consts, wqkv=p["w0qkv"], bqkv=p["b0qkv"],
                                       wproj=p["w0proj"], bproj=p["b0proj"]))
        run(build_ffn0_launch(), dict(consts, wg=p["wg0"], wu=p["wu0"], wd=p["wd0"],
                                      sg=p["sg0"], su=p["su0"], sd=p["sd0"]))
        run(build_attn_launch(1), dict(consts, wqkv=p["w1qkv"], bqkv=p["b1qkv"],
                                       wproj=p["w1proj"], bproj=p["b1proj"]))
        run(build_ffn1_launch(), dict(consts, gpT=p["gpT"], upT=p["upT"], gsT=p["gsT"],
                                      usT=p["usT"], dpT=p["dpT"], dsT=p["dsT"]))

    return np.stack([xs[c].T for c in cores]).astype(np.float32)
